# version 01 (final)
# baseline (speedup 1.0000x reference)
import os
import numpy as np
import concourse.bass as bass
import concourse.mybir as mybir
from concourse.bass_utils import run_bass_kernel_spmd

F32 = mybir.dt.float32
BF16 = mybir.dt.bfloat16
AF = mybir.ActivationFunctionType
ALU = mybir.AluOpType

P = 128
NCH = 8
C = 1024
H = 16
HD = 64
TP = 2432
NW = 351
PFX0, PFXW = 1, 32
OWN0, OWNW = 384, 2048
ATT_TILES = [(0, 384), (384, 512), (896, 512), (1408, 512), (1920, 512)]
SL_TILES = [(0, 64), (384, 512), (896, 512), (1408, 512), (1920, 512)]
SL_SPANS = [(0, 64), (384, 2048)]
CMP_OFF = {0: 0, 384: 64, 896: 576, 1408: 1088, 1920: 1600}
EM05 = float(np.exp(-0.5))
EPS = 1e-5

N_SUB = int(os.environ.get("MK_NSUB", "4"))
MK_PH = int(os.environ.get("MK_PH", "99"))
MK_DBG = os.environ.get("MK_DBG", "")


def _phgate(cx, n):
    return n < MK_PH

A_BASE = 20480
A_XHAT = A_BASE
A_XRES = A_XHAT + 38912
A_XRES_END = A_XRES + 77824
A_XNG = A_XRES_END
A_SCR = A_XNG + 33792
A_END = 226700


class Sem:
    def __init__(self, handle):
        self.h = handle
        self.v = 0

    def inc(self, inst, n=1):
        self.v += n
        inst.then_inc(self.h, n)


class Ctx:
    nblk = 0


def blk(cx, name):
    cx.nblk += 1
    cx._dsnap = tuple(s.v for s in
                      (cx.sd, cx.sg, cx.sp, cx.sda, cx.sga, cx.sgb,
                       cx.spa, cx.spb, cx.sra, cx.srb))
    return cx.nc.Block("%s_%d" % (name, cx.nblk))


def dma_waits(cx, eng):
    sems = (cx.sd, cx.sg, cx.sp, cx.sda, cx.sga, cx.sgb, cx.spa, cx.spb,
            cx.sra, cx.srb)
    for s, v in zip(sems, cx._dsnap):
        if v:
            eng.wait_ge(s.h, v)


def pm_dram(handle, d0, d2, t0, W):
    return bass.AP(handle, t0, [[d2, P], [P * d2, d0], [1, W]])


def build_nc():
    nc = bass.Bass("TRN2", target_bir_lowering=False, debug=False)
    cx = Ctx()
    cx.nc = nc
    din = {}
    cx.din = din

    def inp(name, shape):
        din[name] = nc.dram_tensor(name, list(shape), F32, kind="ExternalInput")

    inp("x_fm", [NCH, P, TP])
    for w in ("Wr", "Wk", "Wv", "Wo"):
        inp(w, [NCH, P, C])
    for w in ("w1", "a1", "g1"):
        inp(w, [NCH, P, 64])
    inp("w2", [64, H])
    inp("a2", [64, H])
    inp("g2", [64, C])
    inp("colsin", [P, 10, NCH])
    inp("w0col", [H, 1])
    inp("a0col", [H, 1])
    if N_SUB >= 2:
        inp("W1T", [NCH, P, 4096])
        inp("W2T", [32, P, C])
        inp("b1col", [P, 32])
        inp("b2row", [NCH, 1, P])
        inp("fscale", [1, 1])
    if N_SUB >= 3:
        inp("Wqkv", [NCH, P, 3072])
        inp("Wgate", [NCH, P, C])
        inp("WoutT", [NCH, P, C])
        inp("bqkv_row", [24, 1, P])
        inp("bqkvT", [P, 24])
        inp("boutT", [P, NCH])
        inp("bgate_row", [NCH, 1, P])
        inp("pos_bias", [44, H])

    cx.out_x = nc.dram_tensor("out_x", [NCH, P, OWNW], F32, kind="ExternalOutput")
    cx.dbg = nc.dram_tensor("dbg", [NCH, P, TP], F32, kind="ExternalOutput") \
        if MK_DBG else None

    hind = np.zeros((NCH, P, H), np.float32)
    rind = np.zeros((NCH, H, P), np.float32)
    rind_w = np.zeros((NCH, H, P), np.float32)
    for ch in range(NCH):
        for pp in range(P):
            hh = 2 * ch + pp // 64
            hind[ch, pp, hh] = 1.0
            rind[ch, hh, pp] = 1.0
            rind_w[ch, hh, pp] = EM05
    cx.c_ones128 = nc.inline_tensor(np.ones((P, 1), np.float32), name="c_ones128")
    cx.c_rep1 = nc.inline_tensor(np.ones((33, P), np.float32), name="c_rep1")
    cx.c_hind = nc.inline_tensor(hind, name="c_hind")
    cx.c_rind = nc.inline_tensor(rind, name="c_rind")
    cx.c_rindw = nc.inline_tensor(rind_w, name="c_rindw")
    cx.c_ident = nc.inline_tensor(np.eye(P, dtype=np.float32), name="c_ident")
    m16 = np.zeros((P, 16), np.float32)
    for m in range(P):
        for i4 in range(4):
            m16[m, 4 * i4 + (m % 4)] = 1.0
    cx.c_mask16 = nc.inline_tensor(m16, name="c_mask16")
    rsel = np.zeros((NCH, P, C), np.float32)
    for mc in range(NCH):
        for mr in range(P):
            mglob = 128 * mc + mr
            for o in range(C):
                if mglob // 4 == o % 256:
                    rsel[mc, mr, o] = 1.0
    import ml_dtypes as _mld
    cx.c_rsel = nc.inline_tensor(rsel.astype(_mld.bfloat16), name="c_rsel")
    bmask = np.zeros((H, C), np.float32)
    for i4 in range(4):
        for jp in range(4):
            bmask[4 * i4 + jp, 256 * i4:256 * (i4 + 1)] = 1.0
    cx.c_bmask = nc.inline_tensor(bmask, name="c_bmask")

    if N_SUB >= 3:
        i44 = np.zeros((44, 24), np.float32)
        for i, off in enumerate(OFFS44):
            i44[i, DOFFS.index(off)] = 1.0
        cx.c_i44 = nc.inline_tensor(i44, name="c_i44")
        i512 = np.zeros((24, 512), np.float32)
        for oi, off in enumerate(DOFFS):
            if off <= 256:
                i512[oi, 127 + off] = 1.0
        cx.c_i512 = nc.inline_tensor(i512.astype(_mld.bfloat16), name="c_i512")
        izc = np.zeros((24, 32), np.float32)
        for oi, off in enumerate(DOFFS):
            for n in range(32):
                if off > n + 32:
                    izc[oi, n] = float(np.exp(-4.0))
        cx.c_izc = nc.inline_tensor(izc.astype(_mld.bfloat16), name="c_izc")
        cx.c_id24 = nc.inline_tensor(np.eye(24, dtype=np.float32),
                                     name="c_id24")
        cx.c_id16 = nc.inline_tensor(
            np.eye(H, dtype=np.float32).astype(_mld.bfloat16), name="c_id16")

    AT = nc.alloc_sbuf_tensor_at
    cx.AT = AT
    cx.xhat = AT("xhat", [P, NCH, TP], BF16, offset=A_XHAT)
    cx.x_res = AT("x_res", [P, NCH, TP], F32, offset=A_XRES)
    cx.xng = AT("xng", [P, NCH, 2112], BF16, offset=A_XNG)

    cx._o = A_SCR

    def scr(name, shape, dt):
        t = AT(name, shape, dt, offset=cx._o)
        n = 1
        for s in shape[1:]:
            n *= s
        cx._o += n * (4 if dt == F32 else 2)
        cx._o = (cx._o + 31) // 32 * 32
        return t

    cx.prowA = scr("prowA", [P, TP], F32)
    cx.prowB = scr("prowB", [33, TP], F32)
    cx.mrow = scr("mrow", [33, TP], F32)
    cx.fac_row = scr("fac_row", [H, TP], BF16)
    cx.bfac_row = scr("bfac_row", [H, TP], BF16)
    cx.gn_rstd = scr("gn_rstd", [H, TP], BF16)
    cx.gn_mr = scr("gn_mr", [H, TP], BF16)
    cx.cols = scr("cols", [P, 16, NCH], F32)
    cx.hcols = scr("hcols", [H, 8], F32)
    cx.ones128_sb = scr("ones128_sb", [P, 1], F32)
    cx.rep1_sb = scr("rep1_sb", [33, P], F32)
    cx.hindb_sb = scr("hindb_sb", [P, NCH, H], BF16)
    cx.w2_sb = scr("w2_sb", [64, H], BF16)
    cx.a2_sb = scr("a2_sb", [64, H], BF16)
    cx.wobuf = scr("wobuf", [P, 2, NCH, 128], BF16)
    cx.cconst = scr("cconst", [P, 4], F32)
    cx.ident_bf = scr("ident_bf", [P, P], BF16)
    cx.scr_end = cx._o
    assert cx._o < A_END, cx._o
    cx.lnsq = AT("lnsq", [P, 2, 512], F32, offset=A_XNG)
    cx.lnrep_bf = AT("lnrep_bf", [P, 512], BF16, offset=A_XNG + 4096)
    cx.lnxc = AT("lnxc", [P, 512], BF16, offset=A_XNG + 5120)

    cx.R_AROW, cx.R_NS, cx.R_BC, cx.R_INV = 0, 32, 64, 96

    cx.SLOT = {"mix_r": 0, "mix_w": 1, "mix_k": 2, "mix_v": 3, "mix_a": 4,
               "mix_g": 5, "kk": 6, "rk": 7, "lnw": 8, "lnb": 9,
               "kk2": 10, "kkrk": 11}

    cx.k_hbm = nc.dram_tensor("k_hbm", [NCH, P, TP], BF16)
    cx.v_hbm = nc.dram_tensor("v_hbm", [NCH, P, TP], BF16)
    cx.r_hbm = nc.dram_tensor("r_hbm", [NCH, P, TP], BF16)
    cx.x_spill = nc.dram_tensor("x_spill", [NCH, P, TP], F32)
    cx.f_out = nc.dram_tensor("f_out", [NCH, P, TP], F32)

    cx.pb = [nc.alloc_psum_tensor("pb%d" % i, [P, 512], F32) for i in range(8)]

    for val, col in ((0.0, 0), (1.0, 1), (EPS, 2), (-4.0, 3)):
        nc.const_aps.register(F32, val, cx.cconst[:, col:col + 1])

    cx._semctx = []

    def mksem(name):
        cm = nc.semaphore(name)
        hh = cm.__enter__()
        cx._semctx.append(cm)
        return Sem(hh)

    cx.sd = mksem("sd")
    cx.sg = mksem("sg")
    cx.sp = mksem("sp")
    cx.sv = mksem("sv")
    cx.ss = mksem("ss")
    cx.st = mksem("st")
    cx.sga = mksem("sga")
    cx.sgb = mksem("sgb")
    cx.spa = mksem("spa")
    cx.spb = mksem("spb")
    cx.sra = mksem("sra")
    cx.srb = mksem("srb")
    cx.sda = mksem("sda")
    cx.scc = mksem("scc")
    cx.sgm = mksem("sgm")

    with nc.allow_low_precision(
            reason="bf16 rounding of matmul-operand-class intermediates; "
                   "fp32 accumulation is preserved everywhere it matters"):
        ph_load(cx)
        if MK_PH >= 1:
            ph_ln(cx, ATT_TILES)
        if MK_DBG == "ln1":
            ph_debug_dump(cx, cx.xhat, TP)
        if MK_DBG == "lnrows":
            nc2 = cx.nc
            with nc2.Block("dbgrows") as B:
                @B.gpsimd
                def _(g):
                    dma_waits(cx, g)
                    cx.sg.inc(g.dma_start(
                        out=bass.AP(cx.dbg, 0, [[TP, 33], [1, TP]]),
                        in_=cx.prowB[:, :]), 16)
                    cx.sg.inc(g.dma_start(
                        out=bass.AP(cx.dbg, 33 * TP, [[TP, 33], [1, TP]]),
                        in_=cx.mrow[:, :]), 16)
                    g.wait_ge(cx.sg.h, cx.sg.v)
        if MK_PH >= 2 and MK_DBG != "ln1":
            ph_attna(cx)
        if N_SUB >= 2 and MK_PH >= 29:
            ph_ln(cx, SL_TILES)
            if MK_PH >= 30:
                ph_ffn(cx)
        if N_SUB >= 3:
            ph_ln(cx, SL_TILES)
            ph_attnb(cx)
        if N_SUB >= 4:
            ph_ln(cx, SL_TILES)
            ph_ffn(cx)
        ph_store(cx)

    for cm in reversed(cx._semctx):
        cm.__exit__(None, None, None)
    return nc


def ph_debug_dump(cx, src, width):
    nc = cx.nc
    with blk(cx, "dbgdump") as B:
        @B.gpsimd
        def _(g):
            dma_waits(cx, g)
            cx.sg.inc(g.dma_start(
                out=pm_dram(cx.dbg, NCH, TP, 0, width),
                in_=src[:, :, 0:width]), 16)
            g.wait_ge(cx.sg.h, cx.sg.v)


def ph_load(cx):
    nc, din = cx.nc, cx.din
    with blk(cx, "load") as B:
        @B.sync
        def _(sync):
            cx.sd.inc(sync.dma_start(
                out=cx.x_res[:, :, :],
                in_=pm_dram(din["x_fm"], NCH, TP, 0, TP)), 16)
            cx.sd.inc(sync.dma_start(out=cx.cols[:, 0:10, :],
                                     in_=din["colsin"].ap()), 16)
            cx.sd.inc(sync.dma_start(out=cx.hcols[:, 0:1], in_=din["w0col"].ap()), 16)
            cx.sd.inc(sync.dma_start(out=cx.hcols[:, 1:2], in_=din["a0col"].ap()), 16)
            cx.sd.inc(sync.dma_start(out=cx.ones128_sb[:, :], in_=cx.c_ones128.ap()), 16)
            cx.sd.inc(sync.dma_start(out=cx.rep1_sb[:, :], in_=cx.c_rep1.ap()), 16)

        @B.gpsimd
        def _(g):
            g.memset(cx.cconst[:, 0:1], 0.0)
            g.memset(cx.cconst[:, 1:2], 1.0)
            g.memset(cx.cconst[:, 2:3], EPS)
            g.memset(cx.cconst[:, 3:4], -4.0)
            cx.sg.inc(g.dma_start(
                out=cx.hindb_sb[:, :, :],
                in_=pm_dram(cx.c_hind, NCH, H, 0, H)), 16)
            cx.sg.inc(g.dma_start(out=cx.w2_sb[:, :], in_=din["w2"].ap()), 16)
            cx.sg.inc(g.dma_start(out=cx.ident_bf[:, :], in_=cx.c_ident.ap()), 16)
            cx.sg.inc(g.dma_start(out=cx.a2_sb[:, :], in_=din["a2"].ap()), 16)

        @B.vector
        def _(v):
            v.wait_ge(cx.sd.h, cx.sd.v)
            v.wait_ge(cx.sg.h, cx.sg.v)
            S = cx.SLOT
            cx.sv.inc(v.tensor_mul(cx.cols[:, S["kk2"], :],
                                   cx.cols[:, S["kk"], :],
                                   cx.cols[:, S["kk"], :]))
            cx.sv.inc(v.tensor_mul(cx.cols[:, S["kkrk"], :],
                                   cx.cols[:, S["kk"], :],
                                   cx.cols[:, S["rk"], :]))


def ph_ln(cx, tiles):
    nc = cx.nc
    spans = [(0, TP)] if tiles is ATT_TILES else SL_SPANS
    nt = len(tiles)

    with blk(cx, "ln_stats") as B:
        ss0, st0, sv0 = cx.ss.v, cx.st.v, cx.sv.v

        @B.scalar
        def _(s):
            dma_waits(cx, s)
            idx = 0
            for ti, (t0, W) in enumerate(tiles):
                for ch in range(NCH):
                    if idx >= 2:
                        s.wait_ge(cx.st.h, st0 + 2 * (idx - 1))
                    cx.ss.inc(s.activation(cx.lnsq[:, idx % 2, 0:W],
                                           cx.x_res[:, ch, t0:t0 + W], AF.Square))
                    idx += 1

        @B.tensor
        def _(t):
            dma_waits(cx, t)
            idx = 0
            for ti, (t0, W) in enumerate(tiles):
                bank = cx.pb[ti % 2]
                if ti >= 2:
                    t.wait_ge(cx.sv.h, sv0 + 2 * (ti - 1))
                for ch in range(NCH):
                    t.wait_ge(cx.ss.h, ss0 + idx + 1)
                    t.matmul(bank[0:1, 0:W], cx.ones128_sb[:, 0:1],
                             cx.x_res[:, ch, t0:t0 + W],
                             start=(ch == 0), stop=(ch == NCH - 1))
                    cx.st.inc(t.matmul(bank[32:33, 0:W], cx.ones128_sb[:, 0:1],
                                       cx.lnsq[:, idx % 2, 0:W],
                                       start=(ch == 0), stop=(ch == NCH - 1),
                                       tile_position=(0, 32)), 2)
                    idx += 1

        @B.vector
        def _(v):
            for ti, (t0, W) in enumerate(tiles):
                bank = cx.pb[ti % 2]
                v.wait_ge(cx.st.h, st0 + 16 * (ti + 1))
                cx.sv.inc(v.tensor_copy(cx.prowB[0:1, t0:t0 + W], bank[0:1, 0:W]))
                cx.sv.inc(v.tensor_copy(cx.prowB[32:33, t0:t0 + W],
                                        bank[32:33, 0:W]))

    with blk(cx, "ln_finish") as B:
        nsp = len(spans)
        sv0, ss0, st0 = cx.sv.v, cx.ss.v, cx.st.v
        sv_rows = sv0 + 4 * nsp
        ss_sqrt = ss0 + nsp
        sv_recip = sv_rows + nsp

        @B.vector
        def _(v):
            for (t0, W) in spans:
                sl = slice(t0, t0 + W)
                M = cx.mrow[0:1, sl]
                cx.sv.inc(v.tensor_scalar_mul(M, cx.prowB[0:1, sl], 1.0 / C))
                cx.sv.inc(v.tensor_scalar_mul(cx.prowB[32:33, sl],
                                              cx.prowB[32:33, sl], 1.0 / C))
                cx.sv.inc(v.tensor_mul(cx.prowA[32:33, sl], M, M))
                cx.sv.inc(v.tensor_sub(cx.prowB[32:33, sl], cx.prowB[32:33, sl],
                                       cx.prowA[32:33, sl]))
            v.wait_ge(cx.ss.h, ss_sqrt)
            for (t0, W) in spans:
                sl = slice(t0, t0 + W)
                cx.sv.inc(v.reciprocal(cx.mrow[32:33, sl], cx.prowB[32:33, sl]))
            for ti, (t0, W) in enumerate(tiles):
                v.wait_ge(cx.st.h, st0 + 2 * (ti + 1))
                cx.sv.inc(v.tensor_copy(cx.lnrep_bf[:, 0:W], cx.pb[3][:, 0:W]))
                for ch in range(NCH):
                    cx.sv.inc(v.scalar_tensor_tensor(
                        cx.lnxc[:, 0:W], cx.x_res[:, ch, t0:t0 + W], 1.0,
                        cx.pb[2][:, 0:W], ALU.mult, ALU.subtract))
                    cx.sv.inc(v.tensor_mul(cx.xhat[:, ch, t0:t0 + W],
                                           cx.lnxc[:, 0:W], cx.lnrep_bf[:, 0:W]))

        @B.scalar
        def _(s):
            s.wait_ge(cx.sv.h, sv_rows)
            for (t0, W) in spans:
                sl = slice(t0, t0 + W)
                cx.ss.inc(s.activation(cx.prowB[32:33, sl], cx.prowB[32:33, sl],
                                       AF.Sqrt, bias=EPS))

        @B.tensor
        def _(t):
            t.wait_ge(cx.sv.h, sv_recip)
            for ti, (t0, W) in enumerate(tiles):
                if ti >= 1:
                    t.wait_ge(cx.sv.h, sv_recip + ti * (1 + 2 * NCH))
                t.matmul(cx.pb[2][:, 0:W], cx.rep1_sb[0:1, :],
                         cx.mrow[0:1, t0:t0 + W], start=True, stop=True)
                cx.st.inc(t.matmul(cx.pb[3][:, 0:W], cx.rep1_sb[32:33, :],
                                   cx.mrow[32:33, t0:t0 + W],
                                   start=True, stop=True), 2)


def _emit_sx(cx, v, sx, t0, W):
    if t0 == 0:
        cx.sv.inc(v.memset(sx[:, :, 0:1], 0.0))
        for ch in range(NCH):
            cx.sv.inc(v.tensor_sub(sx[:, ch, 1:W], cx.xhat[:, ch, 1:W],
                                   cx.xhat[:, ch, 0:W - 1]))
    else:
        for ch in range(NCH):
            cx.sv.inc(v.tensor_sub(sx[:, ch, 0:W], cx.xhat[:, ch, t0:t0 + W],
                                   cx.xhat[:, ch, t0 - 1:t0 + W - 1]))


def ph_attna(cx):
    nc = cx.nc
    AT = cx.AT

    o = A_XRES
    Wsb = AT("a_Wsb", [P, NCH, C], BF16, offset=o); o += NCH * C * 2
    sx = AT("a_sx", [P, NCH, 512], BF16, offset=o); o += NCH * 512 * 2
    mixb = AT("a_mixb", [P, NCH, 512], BF16, offset=o); o += NCH * 512 * 2
    ptile = AT("a_ptile", [P, NCH, 512], BF16, offset=o); o += NCH * 512 * 2
    ktile = AT("a_ktile", [P, NCH, 512], BF16, offset=o); o += NCH * 512 * 2
    ebuf = AT("a_ebuf", [P, NCH, 512], BF16, offset=o); o += NCH * 512 * 2
    l1 = AT("a_l1", [P, 3, NCH, 64], BF16, offset=o); o += 3 * NCH * 64 * 2
    tnh = AT("a_tnh", [64, 2, 512], BF16, offset=o); o += 2 * 512 * 2
    lmix = AT("a_lmix", [P, 3, 512], BF16, offset=o); o += 3 * 512 * 2
    A_TOP = A_XRES_END - 4864
    assert o <= A_TOP, o
    tanhxg = AT("a_tanhxg", [64, TP], BF16, offset=A_TOP)
    cx.tanhxg = tanhxg
    cx.w_row = AT("w_row", [H, TP], F32, offset=A_XNG + 6144)

    with blk(cx, "aprep") as B:
        @B.sync
        def _(sync):
            dma_waits(cx, sync)
            cx.sp.inc(sync.dma_start(
                out=pm_dram(cx.x_spill, NCH, TP, 0, TP),
                in_=cx.x_res[:, :, :]), 16)

        @B.gpsimd
        def _(g):
            g.wait_ge(cx.sp.h, cx.sp.v)
            for j, w in enumerate(("w1", "a1", "g1")):
                cx.sg.inc(g.dma_start(
                    out=l1[:, j, :, :],
                    in_=pm_dram(cx.din[w], NCH, 64, 0, 64)), 16)

    if MK_PH < 3:
        return
    _attna_lora(cx, sx, l1, tnh, lmix)
    if MK_PH < 4:
        return
    for i, pj in enumerate(("k", "r", "v")):
        if MK_PH < 4 + i + 1:
            return
        _attna_proj_pass(cx, pj, Wsb, sx, mixb, ptile, ktile, ebuf)
    if MK_PH < 8:
        return

    with blk(cx, "arows") as B:
        ss0 = cx.ss.v

        @B.scalar
        def _(s):
            dma_waits(cx, s)
            cx.ss.inc(s.activation(cx.prowA[32:48, :], cx.prowA[32:48, :],
                                   AF.Sqrt))

        @B.vector
        def _(v):
            v.wait_ge(cx.ss.h, ss0 + 1)
            ns = cx.prowA[32:48, :]
            inv = cx.mrow[0:16, :]
            cx.sv.inc(v.tensor_scalar_max(ns, ns, 1e-12))
            cx.sv.inc(v.reciprocal(inv, ns))
            cx.sv.inc(v.tensor_mul(cx.fac_row[:, :], cx.prowA[0:16, :], inv))
            cx.sv.inc(v.tensor_mul(cx.bfac_row[:, :], cx.prowB[0:16, :], inv))

    o2 = A_XRES
    kch = AT("a_kch", [P, TP], BF16, offset=o2); o2 += TP * 2
    vch = AT("a_vch", [P, TP], BF16, offset=o2); o2 += TP * 2
    rch = AT("a_rch", [P, TP], BF16, offset=o2); o2 += TP * 2
    kkvb = AT("a_kkvb", [P, TP], BF16, offset=o2); o2 += TP * 2
    kvch = AT("a_kvch", [P, TP], F32, offset=o2); o2 += TP * 4
    wch = AT("a_wch", [P, TP], F32, offset=o2); o2 += TP * 4
    stch = AT("a_stch", [P, TP], F32, offset=o2); o2 += TP * 4
    sqch = AT("a_sqch", [P, 512], F32, offset=o2); o2 += 512 * 4
    gch = AT("a_gch", [P, 2112], BF16, offset=o2); o2 += 2112 * 2
    t2ch = AT("a_t2ch", [P, 512], BF16, offset=o2); o2 += 512 * 2
    t3ch = AT("a_t3ch", [P, 512], BF16, offset=o2); o2 += 512 * 2
    rind_sb = AT("a_rind", [H, NCH, P], BF16, offset=o2); o2 += NCH * P * 2
    rindw_sb = AT("a_rindw", [H, NCH, P], F32, offset=o2); o2 += NCH * P * 4
    hindf_sb = AT("a_hindf", [P, NCH, H], F32, offset=o2); o2 += NCH * H * 4
    g2_sb = AT("a_g2", [64, C], BF16, offset=o2); o2 += C * 2
    assert o2 <= A_TOP, o2

    with blk(cx, "scanprep") as B:
        @B.gpsimd
        def _(g):
            dma_waits(cx, g)
            cx.sg.inc(g.dma_start(
                out=rind_sb[:, :, :],
                in_=bass.AP(cx.c_rind, 0, [[P, H], [H * P, NCH], [1, P]])), 16)
            cx.sg.inc(g.dma_start(out=g2_sb[:, :], in_=cx.din["g2"].ap()), 16)

        @B.sync
        def _(sync):
            dma_waits(cx, sync)
            cx.sd.inc(sync.dma_start(
                out=rindw_sb[:, :, :],
                in_=bass.AP(cx.c_rindw, 0, [[P, H], [H * P, NCH], [1, P]])), 16)
            cx.sd.inc(sync.dma_start(
                out=hindf_sb[:, :, :],
                in_=pm_dram(cx.c_hind, NCH, H, 0, H)), 16)

    if MK_PH < 9:
        return
    for ch in range(NCH):
        if MK_PH < 9 + ch + 1:
            return
        _attna_scan_chunk(cx, ch, kch, vch, rch, kkvb, kvch, wch, stch, sqch,
                          rind_sb, rindw_sb, hindf_sb)
    if MK_DBG == "xwkv":
        ph_debug_dump(cx, cx.xhat, TP)
        return
    if MK_PH < 18:
        return
    _attna_gnrows(cx)
    if MK_PH < 19:
        return
    for ch in range(NCH):
        if MK_PH < 19 + ch + 1:
            return
        _attna_loop2_chunk(cx, ch, gch, t2ch, t3ch, rind_sb, g2_sb)
    if MK_DBG == "xng":
        ph_debug_dump(cx, cx.xng, 2112)
        return
    if MK_PH < 28:
        return
    _attna_wo(cx)


def _attna_lora(cx, sx, l1, tnh, lmix):
    nc = cx.nc
    SL = cx.SLOT
    with blk(cx, "alora") as B:
        sv0, ss0, st0 = cx.sv.v, cx.ss.v, cx.st.v
        ntil = len(ATT_TILES)
        nsx = [9 if ti == 0 else 8 for ti in range(ntil)]
        sv_mix = []
        acc = sv0
        for ti in range(ntil):
            acc += nsx[ti]
            row = []
            for kc in range(NCH):
                acc += 3
                row.append(acc)
            sv_mix.append(row)
        st_lora = [st0 + 26 * ti + 24 for ti in range(ntil)]
        st_rows = [st0 + 26 * (ti + 1) for ti in range(ntil)]
        ss_tanh = [ss0 + 5 * ti + 3 for ti in range(ntil)]

        @B.vector
        def _(v):
            dma_waits(cx, v)
            for ti, (t0, W) in enumerate(ATT_TILES):
                _emit_sx(cx, v, sx, t0, W)
                for kc in range(NCH):
                    if ti + kc > 0:
                        prev = (st0 + 26 * ti + 3 * kc) if kc > 0 else st_rows[ti - 1]
                        v.wait_ge(cx.st.h, prev)
                    for j, mname in enumerate(("mix_w", "mix_a", "mix_g")):
                        s = SL[mname]
                        cx.sv.inc(v.scalar_tensor_tensor(
                            lmix[:, j, 0:W], sx[:, kc, 0:W],
                            cx.cols[:, s, kc:kc + 1], cx.xhat[:, kc, t0:t0 + W],
                            ALU.mult, ALU.add))

        @B.tensor
        def _(t):
            dma_waits(cx, t)
            for ti, (t0, W) in enumerate(ATT_TILES):
                if ti > 0:
                    t.wait_ge(cx.ss.h, ss_tanh[ti - 1] + 2)
                for kc in range(NCH):
                    t.wait_ge(cx.sv.h, sv_mix[ti][kc])
                    for j in range(3):
                        cx.st.inc(t.matmul(cx.pb[5 + j][0:64, 0:W], l1[:, j, kc, :],
                                           lmix[:, j, 0:W], start=(kc == 0),
                                           stop=(kc == NCH - 1)))
                t.wait_ge(cx.ss.h, ss_tanh[ti])
                t.matmul(cx.pb[5][0:H, 0:W], cx.w2_sb[:, :], tnh[:, 0, 0:W],
                         start=True, stop=True)
                cx.st.inc(t.matmul(cx.pb[6][0:H, 0:W], cx.a2_sb[:, :],
                                   tnh[:, 1, 0:W], start=True, stop=True), 2)

        @B.scalar
        def _(s):
            dma_waits(cx, s)
            for ti, (t0, W) in enumerate(ATT_TILES):
                s.wait_ge(cx.st.h, st_lora[ti])
                cx.ss.inc(s.activation(tnh[:, 0, 0:W], cx.pb[5][0:64, 0:W], AF.Tanh))
                cx.ss.inc(s.activation(tnh[:, 1, 0:W], cx.pb[6][0:64, 0:W], AF.Tanh))
                cx.ss.inc(s.activation(cx.tanhxg[:, t0:t0 + W],
                                       cx.pb[7][0:64, 0:W], AF.Tanh))
                s.wait_ge(cx.st.h, st_rows[ti])
                cx.ss.inc(s.activation(cx.w_row[:, t0:t0 + W], cx.pb[5][0:H, 0:W],
                                       AF.Sigmoid, bias=cx.hcols[:, 0:1]))
                cx.ss.inc(s.activation(cx.prowA[0:H, t0:t0 + W],
                                       cx.pb[6][0:H, 0:W], AF.Sigmoid,
                                       bias=cx.hcols[:, 1:2]))


def _attna_proj_pass(cx, pj, Wsb, sx, mixb, ptile, ktile, ebuf):
    nc = cx.nc
    SL = cx.SLOT
    wname = {"k": "Wk", "r": "Wr", "v": "Wv"}[pj]
    mname = {"k": "mix_k", "r": "mix_r", "v": "mix_v"}[pj]
    spill = {"k": cx.k_hbm, "r": cx.r_hbm, "v": cx.v_hbm}[pj]
    has_row = pj in ("k", "r")

    with blk(cx, "aproj_" + pj) as B:
        sv0, ss0, st0, sp0, sd0, sg0 = (cx.sv.v, cx.ss.v, cx.st.v, cx.sp.v,
                                        cx.sd.v, cx.sg.v)
        ntil = len(ATT_TILES)
        nsx = [9 if ti == 0 else 8 for ti in range(ntil)]
        sv_mix, sv_stt, sv_nsev = [], [], []
        acc = sv0
        for ti in range(ntil):
            acc += nsx[ti] + NCH
            sv_mix.append(acc)
            if has_row:
                row = []
                for oc in range(NCH):
                    acc += 1
                    row.append(acc)
                sv_stt.append(row)
                acc += 1
                sv_nsev.append(acc)
        mm_per = 64 + (NCH if has_row else 0)
        st_proj = [st0 + mm_per * ti + 64 for ti in range(ntil)]
        st_all = [st0 + mm_per * (ti + 1) for ti in range(ntil)]
        ss_evac = [ss0 + NCH * (ti + 1) for ti in range(ntil)]
        sp_spill = [sp0 + 16 * (ti + 1) for ti in range(ntil)]
        sd_reload = [sd0 + 16 * (ti + 1) for ti in range(ntil)]

        @B.gpsimd
        def _(g):
            dma_waits(cx, g)
            cx.sg.inc(g.dma_start(
                out=Wsb[:, :, :],
                in_=pm_dram(cx.din[wname], NCH, C, 0, C)), 16)

        @B.sync
        def _(sync):
            for ti, (t0, W) in enumerate(ATT_TILES):
                if pj == "r":
                    if ti > 0:
                        sync.wait_ge(cx.sv.h, sv_stt[ti - 1][NCH - 1])
                    cx.sd.inc(sync.dma_start(
                        out=ktile[:, :, 0:W],
                        in_=pm_dram(cx.k_hbm, NCH, TP, t0, W)), 16)
                sync.wait_ge(cx.ss.h, ss_evac[ti])
                cx.sp.inc(sync.dma_start(
                    out=pm_dram(spill, NCH, TP, t0, W),
                    in_=ptile[:, :, 0:W]), 16)

        @B.vector
        def _(v):
            v.wait_ge(cx.sg.h, sg0 + 16)
            for ti, (t0, W) in enumerate(ATT_TILES):
                if ti > 0:
                    v.wait_ge(cx.st.h, st_all[ti - 1])
                _emit_sx(cx, v, sx, t0, W)
                for kc in range(NCH):
                    s = SL[mname]
                    cx.sv.inc(v.scalar_tensor_tensor(
                        mixb[:, kc, 0:W], sx[:, kc, 0:W],
                        cx.cols[:, s, kc:kc + 1], cx.xhat[:, kc, t0:t0 + W],
                        ALU.mult, ALU.add))
                if has_row:
                    if pj == "r":
                        v.wait_ge(cx.sd.h, sd_reload[ti])
                    for oc in range(NCH):
                        v.wait_ge(cx.ss.h, ss0 + NCH * ti + oc + 1)
                        if pj == "k":
                            cx.sv.inc(v.scalar_tensor_tensor(
                                ebuf[:, oc, 0:W], ptile[:, oc, 0:W],
                                cx.cols[:, SL["kk2"], oc:oc + 1],
                                ptile[:, oc, 0:W], ALU.mult, ALU.mult))
                        else:
                            cx.sv.inc(v.scalar_tensor_tensor(
                                ebuf[:, oc, 0:W], ptile[:, oc, 0:W],
                                cx.cols[:, SL["kkrk"], oc:oc + 1],
                                ktile[:, oc, 0:W], ALU.mult, ALU.mult))
                    v.wait_ge(cx.st.h, st_all[ti])
                    dst = (cx.prowA[32:48, t0:t0 + W] if pj == "k"
                           else cx.prowB[0:16, t0:t0 + W])
                    cx.sv.inc(v.tensor_copy(dst, cx.pb[0][0:H, 0:W]))

        @B.tensor
        def _(t):
            t.wait_ge(cx.sg.h, sg0 + 16)
            for ti, (t0, W) in enumerate(ATT_TILES):
                if ti > 0:
                    t.wait_ge(cx.ss.h, ss_evac[ti - 1])
                    if has_row:
                        t.wait_ge(cx.sv.h, sv_nsev[ti - 1])
                t.wait_ge(cx.sv.h, sv_mix[ti])
                for kc in range(NCH):
                    for oc in range(NCH):
                        cx.st.inc(t.matmul(cx.pb[oc][:, 0:W],
                                           Wsb[:, kc, 128 * oc:128 * (oc + 1)],
                                           mixb[:, kc, 0:W],
                                           start=(kc == 0), stop=(kc == NCH - 1)))
                if has_row:
                    for oc in range(NCH):
                        t.wait_ge(cx.sv.h, sv_stt[ti][oc])
                        cx.st.inc(t.matmul(cx.pb[0][0:H, 0:W],
                                           cx.hindb_sb[:, oc, :],
                                           ebuf[:, oc, 0:W],
                                           start=(oc == 0), stop=(oc == NCH - 1)))

        @B.scalar
        def _(s):
            s.wait_ge(cx.sg.h, sg0 + 16)
            for ti, (t0, W) in enumerate(ATT_TILES):
                if ti > 0:
                    s.wait_ge(cx.sp.h, sp_spill[ti - 1])
                for oc in range(NCH):
                    s.wait_ge(cx.st.h, st0 + mm_per * ti + 7 * NCH + oc + 1)
                    cx.ss.inc(s.activation(ptile[:, oc, 0:W], cx.pb[oc][:, 0:W],
                                           AF.Identity))


def _attna_scan_chunk(cx, ch, kch, vch, rch, kkvb, kvch, wch, stch, sqch,
                      rind_sb, rindw_sb, hindf_sb):
    nc = cx.nc
    SL = cx.SLOT
    with blk(cx, "ascan") as B:
        sv0, ss0, st0, sd0 = cx.sv.v, cx.ss.v, cx.st.v, cx.sd.v
        ntil = len(ATT_TILES)
        st_rep2 = [st0 + 2 * (ti + 1) for ti in range(ntil)]
        st_bf = [st0 + 2 * ntil + (ti + 1) for ti in range(ntil)]
        sv_cons = []
        acc = sv0 + 1
        for ti in range(ntil):
            acc += 2
            sv_cons.append(acc)
        sv_scan = acc + 2
        sv_p4 = sv_scan + 1
        sv_p56 = [sv_p4 + 2 * (ti + 1) for ti in range(ntil)]

        @B.sync
        def _(sync):
            dma_waits(cx, sync)
            cx.sd.inc(sync.dma_start(
                out=kch[:, :],
                in_=bass.AP(cx.k_hbm, ch * P * TP, [[TP, P], [1, TP]])), 16)
            cx.sd.inc(sync.dma_start(
                out=vch[:, :],
                in_=bass.AP(cx.v_hbm, ch * P * TP, [[TP, P], [1, TP]])), 16)
            cx.sd.inc(sync.dma_start(
                out=rch[:, :],
                in_=bass.AP(cx.r_hbm, ch * P * TP, [[TP, P], [1, TP]])), 16)

        @B.tensor
        def _(t):
            dma_waits(cx, t)
            for ti, (t0, W) in enumerate(ATT_TILES):
                if ti >= 1:
                    t.wait_ge(cx.sv.h, sv_cons[ti - 1])
                t.matmul(cx.pb[5][0:P, 0:W], rindw_sb[:, ch, :],
                         cx.w_row[:, t0:t0 + W], start=True, stop=True)
                cx.st.inc(t.matmul(cx.pb[6][0:P, 0:W], rind_sb[:, ch, :],
                                   cx.fac_row[:, t0:t0 + W],
                                   start=True, stop=True), 2)
            for ti, (t0, W) in enumerate(ATT_TILES):
                if ti == 0:
                    t.wait_ge(cx.sv.h, sv_p4)
                else:
                    t.wait_ge(cx.sv.h, sv_p56[ti - 1])
                cx.st.inc(t.matmul(cx.pb[7][0:P, 0:W], rind_sb[:, ch, :],
                                   cx.bfac_row[:, t0:t0 + W],
                                   start=True, stop=True))
            for ti, (t0, W) in enumerate(ATT_TILES):
                t.wait_ge(cx.ss.h, ss0 + ti + 1)
                t.matmul(cx.pb[ti][0:H, 0:W], cx.hindb_sb[:, ch, :],
                         cx.xhat[:, ch, t0:t0 + W],
                         start=(ch == 0), stop=(ch == NCH - 1))
                cx.st.inc(t.matmul(cx.pb[ti][32:32 + H, 0:W], hindf_sb[:, ch, :],
                                   sqch[:, 0:W], start=(ch == 0),
                                   stop=(ch == NCH - 1), tile_position=(0, 32),
                                   skip_group_check=True), 2)

        @B.vector
        def _(v):
            dma_waits(cx, v)
            v.wait_ge(cx.sd.h, sd0 + 48)
            cx.sv.inc(v.tensor_mul(kkvb[:, :], kch[:, :], vch[:, :]))
            for ti, (t0, W) in enumerate(ATT_TILES):
                v.wait_ge(cx.st.h, st_rep2[ti])
                cx.sv.inc(v.scalar_tensor_tensor(
                    kvch[:, t0:t0 + W], kkvb[:, t0:t0 + W],
                    cx.cols[:, SL["kk"], ch:ch + 1],
                    cx.pb[6][:, 0:W], ALU.mult, ALU.mult))
                cx.sv.inc(v.tensor_copy(wch[:, t0:t0 + W], cx.pb[5][:, 0:W]))
            cx.sv.inc(v.tensor_tensor_scan(stch[:, 0:33], wch[:, 0:33],
                                           kvch[:, 0:33], 0.0, ALU.mult, ALU.add))
            cx.sv.inc(v.tensor_tensor_scan(stch[:, 33:TP], wch[:, 33:TP],
                                           kvch[:, 33:TP], 0.0, ALU.mult, ALU.add))
            cx.sv.inc(v.tensor_mul(kvch[:, :], stch[:, :], rch[:, :]))
            for ti, (t0, W) in enumerate(ATT_TILES):
                v.wait_ge(cx.st.h, st_bf[ti])
                cx.sv.inc(v.scalar_tensor_tensor(
                    wch[:, t0:t0 + W], vch[:, t0:t0 + W], 1.0,
                    cx.pb[7][:, 0:W], ALU.mult, ALU.mult))
                cx.sv.inc(v.tensor_add(cx.xhat[:, ch, t0:t0 + W],
                                       kvch[:, t0:t0 + W], wch[:, t0:t0 + W]))

        @B.scalar
        def _(s):
            dma_waits(cx, s)
            for ti, (t0, W) in enumerate(ATT_TILES):
                s.wait_ge(cx.sv.h, sv_p56[ti])
                if ti > 0:
                    s.wait_ge(cx.st.h, st0 + 3 * ntil + 2 * ti)
                cx.ss.inc(s.activation(sqch[:, 0:W], cx.xhat[:, ch, t0:t0 + W],
                                       AF.Square))


def _attna_gnrows(cx):
    nc = cx.nc
    with blk(cx, "agnrows") as B:
        sv0, ss0 = cx.sv.v, cx.ss.v
        ntil = len(ATT_TILES)
        sv_pre = [sv0 + 4 * (ti + 1) for ti in range(ntil)]
        ss_sq = [ss0 + (ti + 1) for ti in range(ntil)]

        @B.vector
        def _(v):
            dma_waits(cx, v)
            for ti, (t0, W) in enumerate(ATT_TILES):
                bank = cx.pb[ti]
                cx.sv.inc(v.tensor_scalar_mul(cx.prowA[0:16, t0:t0 + W],
                                              bank[0:H, 0:W], 1.0 / HD))
                cx.sv.inc(v.tensor_scalar_mul(bank[32:32 + H, 0:W],
                                              bank[32:32 + H, 0:W], 1.0 / HD))
                cx.sv.inc(v.tensor_mul(cx.prowA[32:48, t0:t0 + W],
                                       cx.prowA[0:16, t0:t0 + W],
                                       cx.prowA[0:16, t0:t0 + W]))
                cx.sv.inc(v.tensor_sub(bank[32:32 + H, 0:W], bank[32:32 + H, 0:W],
                                       cx.prowA[32:48, t0:t0 + W]))
            for ti, (t0, W) in enumerate(ATT_TILES):
                bank = cx.pb[ti]
                v.wait_ge(cx.ss.h, ss_sq[ti])
                cx.sv.inc(v.reciprocal(cx.gn_rstd[:, t0:t0 + W],
                                       bank[32:32 + H, 0:W]))
                cx.sv.inc(v.tensor_mul(cx.gn_mr[:, t0:t0 + W],
                                       cx.prowA[0:16, t0:t0 + W],
                                       cx.gn_rstd[:, t0:t0 + W]))

        @B.scalar
        def _(s):
            for ti, (t0, W) in enumerate(ATT_TILES):
                bank = cx.pb[ti]
                s.wait_ge(cx.sv.h, sv_pre[ti])
                cx.ss.inc(s.activation(bank[32:32 + H, 0:W], bank[32:32 + H, 0:W],
                                       AF.Sqrt, bias=EPS))


def _attna_loop2_chunk(cx, ch, gch, t2ch, t3ch, rind_sb, g2_sb):
    nc = cx.nc
    SL = cx.SLOT
    with blk(cx, "al2_") as B:
        sv0, ss0, st0 = cx.sv.v, cx.ss.v, cx.st.v
        ntil = len(SL_TILES)
        st_rep = [st0 + 3 * (ti + 1) for ti in range(ntil)]
        ss_g = [ss0 + (ti + 1) for ti in range(ntil)]
        sv_tile = [sv0 + 4 * (ti + 1) for ti in range(ntil)]

        @B.tensor
        def _(t):
            dma_waits(cx, t)
            for ti, (t0, W) in enumerate(SL_TILES):
                if ti >= 1:
                    t.wait_ge(cx.sv.h, sv_tile[ti - 1])
                    t.wait_ge(cx.ss.h, ss_g[ti - 1])
                t.matmul(cx.pb[5][0:P, 0:W], g2_sb[:, 128 * ch:128 * (ch + 1)],
                         cx.tanhxg[:, t0:t0 + W], start=True, stop=True)
                t.matmul(cx.pb[6][0:P, 0:W], rind_sb[:, ch, :],
                         cx.gn_rstd[:, t0:t0 + W], start=True, stop=True)
                cx.st.inc(t.matmul(cx.pb[7][0:P, 0:W], rind_sb[:, ch, :],
                                   cx.gn_mr[:, t0:t0 + W],
                                   start=True, stop=True), 3)

        @B.scalar
        def _(s):
            dma_waits(cx, s)
            for ti, (t0, W) in enumerate(SL_TILES):
                s.wait_ge(cx.st.h, st_rep[ti])
                co = CMP_OFF[t0]
                cx.ss.inc(s.activation(gch[:, co:co + W], cx.pb[5][:, 0:W],
                                       AF.Sigmoid))

        @B.vector
        def _(v):
            dma_waits(cx, v)
            for ti, (t0, W) in enumerate(SL_TILES):
                v.wait_ge(cx.st.h, st_rep[ti])
                v.wait_ge(cx.ss.h, ss_g[ti])
                co = CMP_OFF[t0]
                cx.sv.inc(v.tensor_mul(t2ch[:, 0:W],
                                       cx.xhat[:, ch, t0:t0 + W], cx.pb[6][:, 0:W]))
                cx.sv.inc(v.scalar_tensor_tensor(
                    t3ch[:, 0:W], t2ch[:, 0:W], 1.0,
                    cx.pb[7][:, 0:W], ALU.mult, ALU.subtract))
                cx.sv.inc(v.tensor_scalar(
                    t2ch[:, 0:W], t3ch[:, 0:W],
                    cx.cols[:, SL["lnw"], ch:ch + 1],
                    cx.cols[:, SL["lnb"], ch:ch + 1],
                    ALU.mult, ALU.add))
                cx.sv.inc(v.tensor_mul(cx.xng[:, ch, co:co + W],
                                       t2ch[:, 0:W], gch[:, co:co + W]))


def _attna_wo(cx):
    nc = cx.nc
    with blk(cx, "awo") as B:
        sv0, ss0, st0, sd0, sg0 = cx.sv.v, cx.ss.v, cx.st.v, cx.sd.v, cx.sg.v
        sga0, sgb0 = cx.sga.v, cx.sgb.v
        ntil = len(SL_TILES)
        st_mm = {}
        n = st0
        for oc in range(NCH):
            for ti in range(ntil):
                n += NCH
                st_mm[(oc, ti)] = n
        sv_ev = {}
        n = sv0
        for oc in range(NCH):
            for ti in range(ntil):
                n += 1
                sv_ev[(oc, ti)] = n

        @B.sync
        def _(sync):
            dma_waits(cx, sync)
            cx.sd.inc(sync.dma_start(
                out=cx.x_res[:, :, :],
                in_=pm_dram(cx.x_spill, NCH, TP, 0, TP)), 16)

        @B.gpsimd
        def _(g):
            dma_waits(cx, g)
            for oc in range(NCH):
                if oc >= 2:
                    g.wait_ge(cx.st.h, st_mm[(oc - 2, ntil - 1)])
                slot = cx.sga if oc % 2 == 0 else cx.sgb
                slot.inc(g.dma_start(
                    out=cx.wobuf[:, oc % 2, :, :],
                    in_=pm_dram(cx.din["Wo"], NCH, C, 128 * oc, 128)), 16)

        @B.tensor
        def _(t):
            for oc in range(NCH):
                slot0 = sga0 if oc % 2 == 0 else sgb0
                sem = cx.sga if oc % 2 == 0 else cx.sgb
                t.wait_ge(sem.h, slot0 + 16 * (oc // 2 + 1))
                for ti, (t0, W) in enumerate(SL_TILES):
                    if oc >= 1:
                        t.wait_ge(cx.sv.h, sv_ev[(oc - 1, ti)])
                    for kc in range(NCH):
                        co = CMP_OFF[t0]
                        cx.st.inc(t.matmul(cx.pb[ti][:, 0:W],
                                           cx.wobuf[:, oc % 2, kc, :],
                                           cx.xng[:, kc, co:co + W],
                                           start=(kc == 0), stop=(kc == NCH - 1)))

        @B.vector
        def _(v):
            v.wait_ge(cx.sd.h, sd0 + 16)
            for oc in range(NCH):
                for ti, (t0, W) in enumerate(SL_TILES):
                    v.wait_ge(cx.st.h, st_mm[(oc, ti)])
                    cx.sv.inc(v.tensor_add(cx.x_res[:, oc, t0:t0 + W],
                                           cx.x_res[:, oc, t0:t0 + W],
                                           cx.pb[ti][:, 0:W]))


FF_TILES = [(0, 64)] + [(384 + 256 * k, 256) for k in range(8)]


def ph_ffn(cx):
    nc = cx.nc
    AT = cx.AT
    cx.ffn_call = getattr(cx, "ffn_call", 0) + 1
    sfx = "%d" % cx.ffn_call

    def AT_(name, *a, **k):
        return AT(name + sfx, *a, **k)
    W1sb = AT_("f_W1sb", [P, NCH, 4096], BF16, offset=A_XRES)
    W2sb = AT_("f_W2sb", [P, 32, C], BF16, offset=A_XRES + 65536)
    o = A_SCR + 19456
    Esc = AT_("f_Esc", [4, 4096], BF16, offset=o); o += 4096 * 2
    lhsTu = AT_("f_lhsTu", [P, 32, H], BF16, offset=o); o += 32 * H * 2
    o = A_SCR + 29184
    def _al(x):
        return (x + 31) // 32 * 32
    Etil = AT_("f_Etil", [H, C], BF16, offset=o); o = _al(o + C * 2)
    su = AT_("f_su", [H, 512], BF16, offset=o); o = _al(o + 512 * 2)
    t_sb = AT_("f_t", [4, 512], BF16, offset=o); o = _al(o + 512 * 2)
    h_tr = AT_("f_h", [P, 512], BF16, offset=o); o = _al(o + 512 * 2)
    b2_sb = AT_("f_b2", [1, NCH, P], BF16, offset=o); o = _al(o + NCH * P * 2)
    b1_sb = AT_("f_b1", [P, 32], F32, offset=o); o = _al(o + 32 * 4)
    onesw = AT_("f_onesw", [1, 512], BF16, offset=o); o = _al(o + 512 * 2)
    otile = AT_("f_otile", [P, 2, 256], F32, offset=o); o = _al(o + 2 * 256 * 4)
    fscol = AT_("f_fscol", [P, 1], F32, offset=o); o = _al(o + 4)
    fs1 = AT_("f_fs1", [1, 1], F32, offset=o); o = _al(o + 32)
    pfxT = AT_("f_pfxT", [H, C], BF16, offset=o); o = _al(o + C * 2)
    mask16 = AT_("f_mask16", [P, H], F32, offset=o); o = _al(o + H * 4)
    rsel_sb = AT_("f_rsel", [P, 2, C], BF16, offset=o); o = _al(o + 2 * C * 2)
    bmask_sb = AT_("f_bmask", [H, C], BF16, offset=o); o = _al(o + C * 2)
    lhsTe = AT_("f_lhsTe", [P, NCH, H], BF16, offset=o); o = _al(o + NCH * H * 2)
    assert o <= A_SCR + 48640, o
    pfx_d = nc.dram_tensor("pfx_d%d" % cx.st.v, [H, C], BF16)

    with blk(cx, "ffprep") as B:
        sg0, sp0, st0, sv0, sd0 = cx.sg.v, cx.sp.v, cx.st.v, cx.sv.v, cx.sd.v
        sda0, sra0, srb0 = cx.sda.v, cx.sra.v, cx.srb.v
        st_tr = st0 + 8
        st_fs = st_tr + 1
        st_etil = [st_fs + 2 * (mc + 1) for mc in range(NCH)]
        sv_pfxT = sv0 + 4
        sv_lhse = sv_pfxT + 32 + NCH
        sda_all = sda0 + 48
        sd_pfx = sd0 + 16

        @B.tensor
        def _(t):
            dma_waits(cx, t)
            for ch in range(NCH):
                bank = cx.pb[6 + ch // 4]
                out_bf = bank[0:H, 64 * (ch % 4):64 * (ch % 4) + 64].bitcast(BF16)
                cx.st.inc(t.transpose(out_bf, cx.xhat[:, ch, 2:18],
                                      cx.ident_bf[:, :]))
            t.wait_ge(cx.sv.h, sv_pfxT)
            t.wait_ge(cx.sda.h, sda_all)
            cx.st.inc(t.matmul(cx.pb[5][:, 0:1], cx.rep1_sb[0:1, :],
                               fs1[0:1, 0:1], start=True, stop=True))
            t.wait_ge(cx.sv.h, sv_lhse)
            for mc in range(NCH):
                sem0, sem = ((sra0, cx.sra) if mc % 2 == 0 else (srb0, cx.srb))
                t.wait_ge(sem.h, sem0 + 16 * (mc // 2 + 1))
                t.matmul(cx.pb[6][0:H, 0:512], lhsTe[:, mc, :],
                         rsel_sb[:, mc % 2, 0:512],
                         start=(mc == 0), stop=(mc == NCH - 1))
                cx.st.inc(t.matmul(cx.pb[7][0:H, 0:512], lhsTe[:, mc, :],
                                   rsel_sb[:, mc % 2, 512:1024],
                                   start=(mc == 0), stop=(mc == NCH - 1)), 2)

        @B.vector
        def _(v):
            dma_waits(cx, v)
            cx.sv.inc(v.memset(Esc[:, :], 0.0))
            cx.sv.inc(v.memset(lhsTu[:, :, :], 0.0))
            v.wait_ge(cx.st.h, st_tr)
            cx.sv.inc(v.tensor_copy(pfxT[0:H, 0:512],
                                    cx.pb[6][0:H, 0:256].bitcast(BF16)))
            cx.sv.inc(v.tensor_copy(pfxT[0:H, 512:1024],
                                    cx.pb[7][0:H, 0:256].bitcast(BF16)))
            for jc in range(32):
                jpp = jc // 8
                cx.sv.inc(v.tensor_copy(
                    lhsTu[:, jc, jpp::4], cx.xhat[:, jc % 8, 11:18:2]))
            v.wait_ge(cx.sda.h, sda_all)
            for mc in range(NCH):
                cx.sv.inc(v.tensor_mul(
                    lhsTe[:, mc, :].rearrange("p (a b) -> p a b", a=4),
                    cx.xhat[:, mc, 10:17:2].unsqueeze(2).broadcast_to([P, 4, 4]),
                    mask16[:, :].rearrange("p (a b) -> p a b", a=4)))
            v.wait_ge(cx.st.h, st_fs)
            cx.sv.inc(v.tensor_copy(fscol[:, :], cx.pb[5][:, 0:1]))
            v.wait_ge(cx.st.h, st_etil[NCH - 1])
            v.wait_ge(cx.sg.h, sg0 + 64)
            cx.sv.inc(v.tensor_mul(Etil[:, 0:512], cx.pb[6][0:H, 0:512],
                                   bmask_sb[:, 0:512]))
            cx.sv.inc(v.tensor_mul(Etil[:, 512:1024], cx.pb[7][0:H, 0:512],
                                   bmask_sb[:, 512:1024]))

        @B.gpsimd
        def _(g):
            dma_waits(cx, g)
            g.memset(onesw[:, :], 1.0)
            g.wait_ge(cx.sp.h, sp0 + 16)
            cx.sg.inc(g.dma_start(out=W1sb[:, :, :],
                                  in_=pm_dram(cx.din["W1T"], NCH, 4096, 0, 4096)), 16)
            cx.sg.inc(g.dma_start(out=W2sb[:, :, :],
                                  in_=pm_dram(cx.din["W2T"], 32, C, 0, C)), 16)
            cx.sg.inc(g.dma_start(out=b2_sb[:, :, :],
                                  in_=bass.AP(cx.din["b2row"], 0,
                                              [[P, 1], [P, NCH], [1, P]])), 16)
            cx.sg.inc(g.dma_start(out=bmask_sb[:, :], in_=cx.c_bmask.ap()), 16)

        @B.sync
        def _(sync):
            dma_waits(cx, sync)
            cx.sp.inc(sync.dma_start(
                out=pm_dram(cx.x_spill, NCH, TP, 0, TP),
                in_=cx.x_res[:, :, :]), 16)
            cx.sda.inc(sync.dma_start(out=b1_sb[:, :],
                                      in_=cx.din["b1col"].ap()), 16)
            cx.sda.inc(sync.dma_start(out=fs1[:, :], in_=cx.din["fscale"].ap()), 16)
            cx.sda.inc(sync.dma_start(out=mask16[:, :], in_=cx.c_mask16.ap()), 16)
            sync.wait_ge(cx.sv.h, sv_pfxT)
            cx.sd.inc(sync.dma_start(
                out=bass.AP(pfx_d, 0, [[C, H], [1, C]]),
                in_=pfxT[0:H, 0:C]), 16)
            sync.wait_ge(cx.sd.h, sd_pfx)
            for i in range(4):
                cx.sd.inc(sync.dma_start(
                    out=Esc[i:i + 1, 1024 * i:1024 * (i + 1)],
                    in_=bass.AP(pfx_d, 2 * i * C, [[C, 1], [1, C]])), 16)
            for mc in range(NCH):
                if mc >= 2:
                    sync.wait_ge(cx.st.h, st_fs + 2 * (mc - 1))
                sem = cx.sra if mc % 2 == 0 else cx.srb
                sem.inc(sync.dma_start(
                    out=rsel_sb[:, mc % 2, :],
                    in_=bass.AP(cx.c_rsel, mc * P * C, [[C, P], [1, C]])), 16)


    if MK_DBG == "etil":
        with blk(cx, "dbge") as B:
            @B.gpsimd
            def _(g):
                dma_waits(cx, g)
                cx.sg.inc(g.dma_start(
                    out=bass.AP(cx.dbg, 0, [[TP, H], [1, C]]),
                    in_=Etil[:, :]), 16)
                cx.sg.inc(g.dma_start(
                    out=bass.AP(cx.dbg, H * TP, [[TP, P], [1, 2 * C]]),
                    in_=rsel_sb[:, :, :]), 16)
                cx.sg.inc(g.dma_start(
                    out=bass.AP(cx.dbg, (H + P) * TP, [[TP, P], [1, NCH * H]]),
                    in_=lhsTe[:, :, :]), 16)
                cx.sg.inc(g.dma_start(
                    out=bass.AP(cx.dbg, (H + 2 * P) * TP, [[TP, P], [1, H]]),
                    in_=mask16[:, :]), 16)
                g.wait_ge(cx.sg.h, cx.sg.v)
        return

    for ti, (t0, W) in enumerate(FF_TILES):
        if MK_PH < 31 + ti:
            return
        _ffn_tile(cx, ti, t0, W, W1sb, W2sb, Esc, lhsTu, Etil, su, t_sb, h_tr,
                  b2_sb, b1_sb, onesw, otile, fscol)
    if MK_PH < 45:
        return

    _combine_post(cx, "f" + sfx)


def _combine_post(cx, tag):
    nc = cx.nc
    fscr = cx.AT("c_fscr" + tag, [P, 2, 2112], F32, offset=A_XRES_END)
    with blk(cx, "cpost") as B:
        sd0, sv0 = cx.sd.v, cx.sv.v
        spa0, spb0 = cx.spa.v, cx.spb.v

        @B.sync
        def _(sync):
            dma_waits(cx, sync)
            for kc in range(NCH):
                cx.sd.inc(sync.dma_start(
                    out=cx.x_res[:, kc, :],
                    in_=bass.AP(cx.x_spill, kc * P * TP, [[TP, P], [1, TP]])), 16)
            for kc in range(NCH):
                if kc >= 2:
                    sync.wait_ge(cx.sv.h, sv0 + 2 * (kc - 1))
                sem = cx.spa if kc % 2 == 0 else cx.spb
                sem.inc(sync.dma_start(
                    out=fscr[:, kc % 2, 0:64],
                    in_=bass.AP(cx.f_out, kc * P * TP, [[TP, P], [1, 64]])), 16)
                sem.inc(sync.dma_start(
                    out=fscr[:, kc % 2, 64:2112],
                    in_=bass.AP(cx.f_out, kc * P * TP + 384,
                                [[TP, P], [1, 2048]])), 16)

        @B.vector
        def _(v):
            dma_waits(cx, v)
            v.wait_ge(cx.sd.h, sd0 + 16 * NCH)
            for kc in range(NCH):
                sem0, sem = ((spa0, cx.spa) if kc % 2 == 0
                             else (spb0, cx.spb))
                v.wait_ge(sem.h, sem0 + 32 * (kc // 2 + 1))
                cx.sv.inc(v.tensor_add(cx.x_res[:, kc, 0:64],
                                       cx.x_res[:, kc, 0:64],
                                       fscr[:, kc % 2, 0:64]))
                cx.sv.inc(v.tensor_add(cx.x_res[:, kc, 384:TP],
                                       cx.x_res[:, kc, 384:TP],
                                       fscr[:, kc % 2, 64:2112]))


def _ffn_tile(cx, ti, t0, W, W1sb, W2sb, Esc, lhsTu, Etil, su, t_sb, h_tr,
              b2_sb, b1_sb, onesw, otile, fscol):
    nc = cx.nc
    FST = int(os.environ.get("MK_FST", "5"))
    s2, s3, s4, s5 = FST >= 2, FST >= 3, FST >= 4, FST >= 5

    swap = os.environ.get("MK_OSWAP", "0") == "1"

    def obank(oc):
        b = (3 - oc // 2) if swap else (oc // 2)
        return cx.pb[b], slice(256 * (oc % 2), 256 * (oc % 2) + W)

    with blk(cx, "fftile") as B:
        sv0, ss0, st0 = cx.sv.v, cx.ss.v, cx.st.v
        spa0, spb0 = cx.spa.v, cx.spb.v
        perjc = 9 + (9 if s3 else 0)
        st_t = st0 + 8
        st_h = [st_t + perjc * jc + 9 for jc in range(32)]
        st_ou = [st_t + perjc * (jc + 1) for jc in range(32)]
        st_l2 = st_ou[31] + (8 if s4 else 0)
        st_b2 = {oc: st_l2 + (oc + 1 if s4 else 0) for oc in range(NCH)}
        ss_t = ss0 + 1
        ss_g = [ss_t + jc + 1 for jc in range(32)]
        ss_su = ss_g[31] + 1
        sv_ev = {oc: sv0 + oc + 1 for oc in range(NCH)}

        @B.tensor
        def _(t):
            dma_waits(cx, t)
            if FST == 0:
                return
            for kc in range(NCH):
                cx.st.inc(t.matmul(cx.pb[4][0:4, 256:256 + W],
                                   cx.xhat[:, kc, 3:10:2],
                                   cx.xhat[:, kc, t0:t0 + W],
                                   start=(kc == 0), stop=(kc == NCH - 1)))
            t.wait_ge(cx.ss.h, ss_t)
            if not s2:
                return
            for jc in range(32):
                if jc >= 1:
                    t.wait_ge(cx.ss.h, ss_g[jc - 1])
                for kc in range(NCH):
                    cx.st.inc(t.matmul(cx.pb[4][:, 0:W],
                                       W1sb[:, kc, 128 * jc:128 * (jc + 1)],
                                       cx.xhat[:, kc, t0:t0 + W],
                                       start=(kc == 0), stop=False))
                cx.st.inc(t.matmul(cx.pb[4][:, 0:W], Esc[:, 128 * jc:128 * (jc + 1)],
                                   t_sb[:, 0:W], start=False, stop=True))
                if s3:
                    t.wait_ge(cx.ss.h, ss_g[jc])
                    for oc in range(NCH):
                        bank, cols = obank(oc)
                        cx.st.inc(t.matmul(bank[:, cols],
                                           W2sb[:, jc, 128 * oc:128 * (oc + 1)],
                                           h_tr[:, 0:W],
                                           start=(jc == 0 and oc % 2 == 0),
                                           stop=False))
                    cx.st.inc(t.matmul(cx.pb[5][0:H, 0:W], lhsTu[:, jc, :],
                                       h_tr[:, 0:W], start=(jc == 0),
                                       stop=(jc == 31)))
            if not s4:
                return
            t.wait_ge(cx.ss.h, ss_su)
            for oc in range(NCH):
                bank, cols = obank(oc)
                cx.st.inc(t.matmul(bank[:, cols], Etil[:, 128 * oc:128 * (oc + 1)],
                                   su[:, 0:W], start=False, stop=False))
            for oc in range(NCH):
                bank, cols = obank(oc)
                cx.st.inc(t.matmul(bank[:, cols], b2_sb[0:1, oc, :],
                                   onesw[:, 0:W], start=False,
                                   stop=(oc % 2 == 1)))

        @B.scalar
        def _(s):
            dma_waits(cx, s)
            if FST == 0:
                return
            s.wait_ge(cx.st.h, st_t)
            cx.ss.inc(s.activation(t_sb[:, 0:W], cx.pb[4][0:4, 256:256 + W],
                                   AF.Identity, scale=fscol[0:4, 0:1]))
            if not s2:
                return
            for jc in range(32):
                s.wait_ge(cx.st.h, st_h[jc])
                if jc >= 1 and s3:
                    s.wait_ge(cx.st.h, st_ou[jc - 1])
                cx.ss.inc(s.activation(h_tr[:, 0:W], cx.pb[4][:, 0:W],
                                       AF.Gelu, bias=b1_sb[:, jc:jc + 1]))
            if not s3:
                return
            s.wait_ge(cx.st.h, st_ou[31])
            cx.ss.inc(s.activation(su[:, 0:W], cx.pb[5][0:H, 0:W],
                                   AF.Identity, scale=fscol[0:16, 0:1]))

        @B.vector
        def _(v):
            dma_waits(cx, v)
            if not s5:
                return
            for oc in range(NCH):
                v.wait_ge(cx.st.h, st_b2[oc | 1])
                bank, cols = obank(oc)
                if oc >= 2:
                    sem0, sem = ((spa0, cx.spa) if oc % 2 == 0
                                 else (spb0, cx.spb))
                    v.wait_ge(sem.h, sem0 + 16 * (oc // 2))
                cx.sv.inc(v.tensor_copy(otile[:, oc % 2, 0:W], bank[:, cols]))

        @B.sync
        def _(sync):
            dma_waits(cx, sync)
            if not s5:
                return
            for oc in range(NCH):
                sync.wait_ge(cx.sv.h, sv_ev[oc])
                sem = cx.spa if oc % 2 == 0 else cx.spb
                sem.inc(sync.dma_start(
                    out=bass.AP(cx.f_out, oc * P * TP + t0,
                                [[TP, P], [1, W]]),
                    in_=otile[:, oc % 2, 0:W]), 16)


OFFS44 = [(1 << j) * tau for j in range(11) for tau in range(4)]
DOFFS = sorted(set(OFFS44))
SPARSE_J = [3, 4, 6, 8, 12, 16, 24]
NSP = len(SPARSE_J)
KWC = 5120
NBLK = 16
SCL = 0.125

BO_KWIN = A_XRES
BO_QSB = BO_KWIN + 81920
BO_GATE = BO_QSB + 8192
BO_G3FM = BO_GATE + 8192
BO_VSTR = BO_G3FM + 8192
BO_WSTR = BO_VSTR + 12288
BO_BTOEP = BO_WSTR + 4096
BO_PROD = BO_BTOEP + 12288
BO_EBROW = BO_PROD + 2048
BO_RTM = BO_EBROW + 1024
BO_ESB = BO_RTM + 928
BO_PTB = BO_ESB + 1536
BO_ZST = BO_PTB + 1536
BO_ZT = BO_ZST + 256
BO_ZR = BO_ZT + 64
BO_G3TM = BO_ZR + 64
BO_QP = BO_G3TM + 2048
BO_KWP = BO_QP + 512
BO_VWP = BO_KWP + 1024
BO_GATEP = BO_VWP + 2048
BO_BTP = BO_GATEP + 2048
BO_MISC = BO_BTP + 1024
BO_VW = BO_QSB
BO_KSB = BO_PROD
BO_VSB = BO_PROD + 2048
BO_GSB = BO_VSTR + 6144
BO_T1 = 221056


def ph_attnb(cx):
    nc = cx.nc
    AT = cx.AT
    kwin = AT("b_kwin", [P, NCH, KWC], BF16, offset=BO_KWIN)
    q_sb = AT("b_q", [P, NCH, 512], BF16, offset=BO_QSB)
    gate_sb = AT("b_gate", [P, 4, C], BF16, offset=BO_GATE)
    g3fm = AT("b_g3fm", [P, NCH, 512], BF16, offset=BO_G3FM)
    vstr = AT("b_vstr", [P, 6, C], BF16, offset=BO_VSTR)
    wstr = AT("b_wstr", [P, 2, NCH, P], BF16, offset=BO_WSTR)
    btoep = AT("b_btoep", [P, H, 384], BF16, offset=BO_BTOEP)
    prod = AT("b_prod", [P, 2, 512], BF16, offset=BO_PROD)
    ebrow = AT("b_ebrow", [H, 512], BF16, offset=BO_EBROW)
    rtm = AT("b_rtm", [P, 4, NSP, H], BF16, offset=BO_RTM)
    esb = AT("b_esb", [P, 2, 384], BF16, offset=BO_ESB)
    ptb = AT("b_ptb", [P, 2, 384], BF16, offset=BO_PTB)
    zst = AT("b_zst", [P, 4, H], F32, offset=BO_ZST)
    zt = AT("b_zt", [P, H], F32, offset=BO_ZT)
    zr = AT("b_zr", [P, H], F32, offset=BO_ZR)
    g3tm = AT("b_g3tm", [P, C], BF16, offset=BO_G3TM)
    qp_sb = AT("b_qp", [P, NCH, 32], BF16, offset=BO_QP)
    kwp = AT("b_kwp", [P, NCH, 64], BF16, offset=BO_KWP)
    vwp = AT("b_vwp", [64, C], BF16, offset=BO_VWP)
    gatep = AT("b_gatep", [32, C], BF16, offset=BO_GATEP)
    btp = AT("b_btp", [64, H, 32], BF16, offset=BO_BTP)
    o = BO_MISC
    def _al(x):
        return (x + 31) // 32 * 32
    pb44 = AT("b_pb44", [44, H], F32, offset=o); o = _al(o + H * 4)
    expb = AT("b_expb", [44, H], F32, offset=o); o = _al(o + H * 4)
    bsb = AT("b_bsb", [24, H], BF16, offset=o); o = _al(o + H * 2)
    lnbt = AT("b_lnbt", [H, 24], F32, offset=o); o = _al(o + 24 * 4)
    rows_sb = AT("b_rows", [H, 512], BF16, offset=o); o = _al(o + 512 * 2)
    zct = AT("b_zct", [32, H], F32, offset=o); o = _al(o + H * 4)
    bqt = AT("b_bqt", [P, 24], F32, offset=o); o = _al(o + 24 * 4)
    bot = AT("b_bot", [P, NCH], F32, offset=o); o = _al(o + NCH * 4)
    i44 = AT("b_i44", [44, 24], F32, offset=o); o = _al(o + 24 * 4)
    i512 = AT("b_i512", [24, 512], BF16, offset=o); o = _al(o + 512 * 2)
    izc = AT("b_izc", [24, 32], BF16, offset=o); o = _al(o + 32 * 2)
    id24 = AT("b_id24", [24, 24], F32, offset=o); o = _al(o + 24 * 4)
    id16 = AT("b_id16", [H, H], BF16, offset=o); o = _al(o + H * 2)
    onesb = AT("b_onesb", [P, 1], BF16, offset=o); o = _al(o + 2)
    o1r = AT("b_o1r", [1, P], BF16, offset=o); o = _al(o + P * 2)
    lnb_sb = AT("b_lnb", [24, H], F32, offset=o); o = _al(o + H * 4)
    bvrow = AT("b_bvrow", [1, NCH, P], BF16, offset=o); o = _al(o + NCH * P * 2)
    bgrow = AT("b_bgrow", [1, NCH, P], BF16, offset=o); o = _al(o + NCH * P * 2)
    osb = AT("b_osb", [P, 2, 512], F32, offset=BO_VSTR)
    opfx = AT("b_opfx", [P, NCH, 64], F32, offset=BO_VSTR + 4096)
    tmpb_b = AT("b_tmpb", [P, 2, C], BF16, offset=BO_WSTR)
    assert o <= 219648, o
    t1 = AT("b_t1", [P, C], F32, offset=BO_T1)
    vw_sb = AT("b_vw", [P, NCH, C], BF16, offset=BO_VW)
    ksb = AT("b_ksb", [P, 2, 512], BF16, offset=BO_KSB)
    vsb = AT("b_vsb", [P, 2, C], BF16, offset=BO_VSB)
    gsb = AT("b_gsb", [P, 2, C], BF16, offset=BO_GSB)

    cc_k_in = nc.dram_tensor("cc_k_in", [C, 2048], BF16)
    gath_k = nc.dram_tensor("gath_k", [2, C, 2048], BF16)
    cc_v_in = nc.dram_tensor("cc_v_in", [2048, C], BF16)
    gath_v = nc.dram_tensor("gath_v", [2, 2048, C], BF16)
    rows_d = nc.dram_tensor("rows_d", [H, 512], BF16)
    btr_d = nc.dram_tensor("btr_d", [H, P, 512], BF16)
    gate_d = nc.dram_tensor("gate_d", [2048, C], BF16)
    vpad = nc.dram_tensor("vpad", [40 * P, C], BF16)

    bx = {"b_o1r": o1r, "tmpb": tmpb_b, "kwin": kwin, "q_sb": q_sb, "gate_sb": gate_sb, "g3fm": g3fm,
          "vstr": vstr, "wstr": wstr, "btoep": btoep, "prod": prod,
          "ebrow": ebrow, "rtm": rtm, "esb": esb, "ptb": ptb, "zst": zst,
          "zt": zt, "zr": zr, "g3tm": g3tm, "qp_sb": qp_sb, "kwp": kwp,
          "vwp": vwp, "gatep": gatep, "btp": btp, "pb44": pb44, "expb": expb,
          "bsb": bsb, "lnbt": lnbt, "rows_sb": rows_sb, "zct": zct,
          "bqt": bqt, "bot": bot, "i44": i44, "i512": i512, "izc": izc,
          "id24": id24, "id16": id16, "onesb": onesb, "lnb_sb": lnb_sb,
          "bvrow": bvrow, "bgrow": bgrow, "osb": osb, "opfx": opfx,
          "t1": t1, "vw_sb": vw_sb, "ksb": ksb, "vsb": vsb, "gsb": gsb,
          "cc_k_in": cc_k_in, "gath_k": gath_k, "cc_v_in": cc_v_in,
          "gath_v": gath_v, "rows_d": rows_d, "btr_d": btr_d,
          "gate_d": gate_d, "vpad": vpad}
    cx.bx = bx

    _b_prep(cx)
    if MK_PH < 47:
        return
    _b_kv(cx)
    if MK_PH < 48:
        return
    _b_exch(cx)
    for t in range(4):
        if MK_PH < 49 + t:
            return
        _b_tile(cx, t)
    if MK_PH < 53:
        return
    _b_prefix(cx)
    if MK_PH < 54:
        return
    _combine_post(cx, "b")


def _b_prep(cx):
    nc = cx.nc
    bx = cx.bx
    with blk(cx, "bprep") as B:
        sp0, sd0, sda0 = cx.sp.v, cx.sd.v, cx.sda.v
        st0, sv0, ss0 = cx.st.v, cx.sv.v, cx.ss.v
        sda_all = sda0 + 8 * 16
        ss_exp = ss0 + 1
        st_b = st0 + 1
        ss_ln = ss0 + 2
        sv_bsb = sv0 + 2
        st_rows = st0 + 4
        sv_rows = sv0 + 4
        sd_rows = sd0 + 16
        sd_rep = sd_rows + 16 * H
        sd_btr = sd_rep + 16 * (2 * H + H)

        @B.sync
        def _(sync):
            dma_waits(cx, sync)
            cx.sp.inc(sync.dma_start(
                out=pm_dram(cx.x_spill, NCH, TP, 0, TP),
                in_=cx.x_res[:, :, :]), 16)
            cx.sda.inc(sync.dma_start(out=bx["pb44"][:, :],
                                      in_=cx.din["pos_bias"].ap()), 16)
            cx.sda.inc(sync.dma_start(out=bx["bqt"][:, :],
                                      in_=cx.din["bqkvT"].ap()), 16)
            cx.sda.inc(sync.dma_start(out=bx["bot"][:, :],
                                      in_=cx.din["boutT"].ap()), 16)
            cx.sda.inc(sync.dma_start(out=bx["i44"][:, :],
                                      in_=cx.c_i44.ap()), 16)
            cx.sda.inc(sync.dma_start(out=bx["i512"][:, :],
                                      in_=cx.c_i512.ap()), 16)
            cx.sda.inc(sync.dma_start(out=bx["izc"][:, :],
                                      in_=cx.c_izc.ap()), 16)
            cx.sda.inc(sync.dma_start(out=bx["id24"][:, :],
                                      in_=cx.c_id24.ap()), 16)
            cx.sda.inc(sync.dma_start(out=bx["id16"][:, :],
                                      in_=cx.c_id16.ap()), 16)
            sync.wait_ge(cx.sv.h, sv_rows)
            cx.sd.inc(sync.dma_start(out=rows_dram_ap(bx["rows_d"]),
                                     in_=bx["rows_sb"][:, :]), 16)
            sync.wait_ge(cx.sd.h, sd_rows)
            for h in range(H):
                cx.sd.inc(sync.dma_start(
                    out=bass.AP(bx["btr_d"], h * P * 512, [[512, P], [1, 512]]),
                    in_=bass.AP(bx["rows_d"], h * 512, [[0, P], [1, 512]])), 16)
            sync.wait_ge(cx.sd.h, sd_rep)
            for h in range(H):
                cx.sd.inc(sync.dma_start(
                    out=bx["btoep"][:, h, 0:256],
                    in_=bass.AP(bx["btr_d"], h * P * 512 + 127,
                                [[511, P], [1, 256]])), 16)
                cx.sd.inc(sync.dma_start(
                    out=bx["btoep"][:, h, 256:384],
                    in_=bass.AP(bx["btr_d"], h * P * 512 + 383,
                                [[511, P], [1, 128]])), 16)
                cx.sd.inc(sync.dma_start(
                    out=bx["btp"][:, h, :],
                    in_=bass.AP(bx["btr_d"], h * P * 512 + 159,
                                [[511, 64], [1, 32]])), 16)

        @B.gpsimd
        def _(g):
            dma_waits(cx, g)
            g.memset(bx["onesb"][:, :], 1.0)
            g.memset(bx["b_o1r"][:, :], 1.0)
            g.memset(bx["kwp"][:, :, 0:32], 0.0)
            g.memset(bx["vwp"][0:32, :], 0.0)
            cx.sg.inc(g.dma_start(
                out=bx["bvrow"][:, :, :],
                in_=bass.AP(cx.din["bqkv_row"], 16 * P,
                            [[P, 1], [P, NCH], [1, P]])), 16)
            cx.sg.inc(g.dma_start(
                out=bx["bgrow"][:, :, :],
                in_=bass.AP(cx.din["bgate_row"], 0,
                            [[P, 1], [P, NCH], [1, P]])), 16)
            g.wait_ge(cx.sp.h, sp0 + 16)
            g.memset(bx["kwin"][:, :, :], 0.0)

        @B.scalar
        def _(s):
            s.wait_ge(cx.sda.h, sda_all)
            cx.ss.inc(s.activation(bx["expb"][:, :], bx["pb44"][:, :], AF.Exp))
            s.wait_ge(cx.st.h, st_b)
            cx.ss.inc(s.activation(bx["lnb_sb"][:, :], cx.pb[0][0:24, 0:H],
                                   AF.Ln))

        @B.tensor
        def _(t):
            t.wait_ge(cx.ss.h, ss_exp)
            cx.st.inc(t.matmul(cx.pb[0][0:24, 0:H], bx["i44"][:, :],
                               bx["expb"][:, :], start=True, stop=True))
            t.wait_ge(cx.sv.h, sv_bsb)
            t.wait_ge(cx.ss.h, ss_ln)
            cx.st.inc(t.matmul(cx.pb[1][0:H, 0:24], bx["lnb_sb"][:, :],
                               bx["id24"][:, :], start=True, stop=True))
            cx.st.inc(t.matmul(cx.pb[2][0:H, 0:512], bx["bsb"][:, :],
                               bx["i512"][:, :], start=True, stop=True))
            cx.st.inc(t.matmul(cx.pb[3][0:32, 0:H], bx["izc"][:, :],
                               bx["bsb"][:, :], start=True, stop=True))

        @B.vector
        def _(v):
            v.wait_ge(cx.st.h, st_b)
            cx.sv.inc(v.tensor_copy(bx["bsb"][:, :], cx.pb[0][0:24, 0:H]))
            cx.sv.inc(v.memset(bx["zst"][:, :, :], 0.0))
            v.wait_ge(cx.st.h, st_rows)
            cx.sv.inc(v.tensor_scalar_add(bx["lnbt"][:, :],
                                          cx.pb[1][0:H, 0:24], -4.0))
            cx.sv.inc(v.tensor_copy(bx["rows_sb"][:, :], cx.pb[2][0:H, 0:512]))
            cx.sv.inc(v.tensor_copy(bx["zct"][:, :], cx.pb[3][0:32, 0:H]))


def rows_dram_ap(rows_d):
    return bass.AP(rows_d, 0, [[512, H], [1, 512]])


KV_SEGS = [(384 + 512 * t, 512) for t in range(4)] + [(1, 32)]


def _b_kv(cx):
    nc = cx.nc
    bx = cx.bx
    wstr, ksb, vsb, gsb = bx["wstr"], bx["ksb"], bx["vsb"], bx["gsb"]
    vw, kwp, vwp, gatep = bx["vw_sb"], bx["kwp"], bx["vwp"], bx["gatep"]

    with blk(cx, "bkv_k") as B:
        st0, ss0, sv0 = cx.st.v, cx.ss.v, cx.sv.v
        sga0, sgb0 = cx.sga.v, cx.sgb.v
        spa0, spb0 = cx.spa.v, cx.spb.v
        nseg = len(KV_SEGS)
        st_g = {}
        n = st0
        for dk in range(NCH):
            for sg_ in range(nseg):
                n += 8
                st_g[(dk, sg_)] = n
        ss_e = {}
        n = ss0
        for dk in range(NCH):
            for sg_ in range(nseg):
                n += 1
                ss_e[(dk, sg_)] = n

        @B.gpsimd
        def _(g):
            dma_waits(cx, g)
            for dk in range(NCH):
                if dk >= 2:
                    g.wait_ge(cx.st.h, st_g[(dk - 2, nseg - 1)])
                sem = cx.sga if dk % 2 == 0 else cx.sgb
                sem.inc(g.dma_start(
                    out=wstr[:, dk % 2, :, :],
                    in_=pm_dram(cx.din["Wqkv"], NCH, 3072, C + 128 * dk, 128)),
                    16)

        @B.tensor
        def _(t):
            dma_waits(cx, t)
            for dk in range(NCH):
                sem0, sem = ((sga0, cx.sga) if dk % 2 == 0
                             else (sgb0, cx.sgb))
                t.wait_ge(sem.h, sem0 + 16 * (dk // 2 + 1))
                for sg_, (t0, W) in enumerate(KV_SEGS):
                    idx = dk * nseg + sg_
                    if idx >= 2:
                        t.wait_ge(cx.ss.h, ss_e[divmod(idx - 2, nseg)])
                    for kc in range(NCH):
                        cx.st.inc(t.matmul(
                            cx.pb[idx % 2][:, 0:W], wstr[:, dk % 2, kc, :],
                            cx.xhat[:, kc, t0:t0 + W],
                            start=(kc == 0), stop=(kc == NCH - 1)))

        @B.scalar
        def _(s):
            dma_waits(cx, s)
            for dk in range(NCH):
                for sg_, (t0, W) in enumerate(KV_SEGS):
                    idx = dk * nseg + sg_
                    s.wait_ge(cx.st.h, st_g[(dk, sg_)])
                    if sg_ == nseg - 1:
                        cx.ss.inc(s.activation(
                            kwp[:, dk, 32:64], cx.pb[idx % 2][:, 0:32],
                            AF.Identity, bias=bx["bqt"][:, 8 + dk:9 + dk]))
                        continue
                    sidx = dk * (nseg - 1) + sg_
                    if sidx >= 2:
                        sem0, sem = ((spa0, cx.spa) if sidx % 2 == 0
                                     else (spb0, cx.spb))
                        s.wait_ge(sem.h, sem0 + 16 * (sidx // 2))
                    cx.ss.inc(s.activation(
                        ksb[:, sidx % 2, 0:W], cx.pb[idx % 2][:, 0:W],
                        AF.Identity, bias=bx["bqt"][:, 8 + dk:9 + dk]))

        @B.sync
        def _(sync):
            dma_waits(cx, sync)
            for dk in range(NCH):
                for sg_, (t0, W) in enumerate(KV_SEGS):
                    if sg_ == nseg - 1:
                        continue
                    sidx = dk * (nseg - 1) + sg_
                    sync.wait_ge(cx.ss.h, ss_e[(dk, sg_)])
                    sem = cx.spa if sidx % 2 == 0 else cx.spb
                    sem.inc(sync.dma_start(
                        out=bass.AP(bx["cc_k_in"], 128 * dk * 2048 + 512 * sg_,
                                    [[2048, P], [1, 512]]),
                        in_=ksb[:, sidx % 2, 0:512]), 16)

    _b_tm_pass(cx, "v", 2048, vw, vsb, bx["bvrow"], None, bx["cc_v_in"],
               vwp, 32)
    _b_tm_pass(cx, "g", 0, vw, gsb, bx["bgrow"], AF.Sigmoid, bx["gate_d"],
               gatep, 0)


def _b_tm_pass(cx, tag, wcol, vw, vsb, brow, act, out_d, pfx_dst, pfx_base):
    nc = cx.nc
    bx = cx.bx
    with blk(cx, "bkv" + tag) as B:
        st0, ss0, sv0, sg0 = cx.st.v, cx.ss.v, cx.sv.v, cx.sg.v
        spa0, spb0 = cx.spa.v, cx.spb.v
        st_g = {}
        n = st0
        for vb in range(NBLK + 1):
            n += 18
            st_g[vb] = n
        ev = {}
        n = sv0 if act is None else ss0
        for vb in range(NBLK + 1):
            n += 2
            ev[vb] = n

        @B.gpsimd
        def _(g):
            dma_waits(cx, g)
            if wcol:
                src = pm_dram(cx.din["Wqkv"], NCH, 3072, wcol, C)
            else:
                src = pm_dram(cx.din["Wgate"], NCH, C, 0, C)
            cx.sg.inc(g.dma_start(out=vw[:, :, :], in_=src), 16)

        @B.tensor
        def _(t):
            dma_waits(cx, t)
            t.wait_ge(cx.sg.h, sg0 + 16)
            eng = cx.ss if act is not None else cx.sv
            for vb in range(NBLK + 1):
                pbb = 2 + 2 * (vb % 2)
                if vb >= 2:
                    t.wait_ge(eng.h, ev[vb - 2])
                if vb < NBLK:
                    xsl = cx.xhat[:, :, 384 + 128 * vb:384 + 128 * (vb + 1)]
                    M = 128
                else:
                    xsl = cx.xhat[:, :, 1:33]
                    M = 32
                for half in range(2):
                    for kc in range(NCH):
                        t.matmul(cx.pb[pbb + half][0:M, 0:512],
                                 xsl[:, kc, :],
                                 vw[:, kc, 512 * half:512 * (half + 1)],
                                 start=(kc == 0), stop=False)
                    cx.st.inc(t.matmul(
                        cx.pb[pbb + half][0:M, 0:512], bx["b_o1r"][:, 0:M],
                        brow[:, 4 * half:4 * (half + 1), :].rearrange(
                            "a b c -> a (b c)"),
                        start=False, stop=True), 9)

        @B.vector
        def _(v):
            dma_waits(cx, v)
            if act is not None:
                return
            for vb in range(NBLK + 1):
                pbb = 2 + 2 * (vb % 2)
                v.wait_ge(cx.st.h, st_g[vb])
                if vb < NBLK:
                    if vb >= 2:
                        sem0, sem = ((spa0, cx.spa) if vb % 2 == 0
                                     else (spb0, cx.spb))
                        v.wait_ge(sem.h, sem0 + 16 * (vb // 2))
                    cx.sv.inc(v.tensor_copy(vsb[:, vb % 2, 0:512],
                                            cx.pb[pbb][:, 0:512]))
                    cx.sv.inc(v.tensor_copy(vsb[:, vb % 2, 512:1024],
                                            cx.pb[pbb + 1][:, 0:512]))
                else:
                    cx.sv.inc(v.tensor_copy(pfx_dst[pfx_base:pfx_base + 32,
                                                    0:512],
                                            cx.pb[pbb][0:32, 0:512]))
                    cx.sv.inc(v.tensor_copy(pfx_dst[pfx_base:pfx_base + 32,
                                                    512:1024],
                                            cx.pb[pbb + 1][0:32, 0:512]))

        @B.scalar
        def _(s):
            dma_waits(cx, s)
            if act is None:
                return
            for vb in range(NBLK + 1):
                pbb = 2 + 2 * (vb % 2)
                s.wait_ge(cx.st.h, st_g[vb])
                if vb < NBLK:
                    if vb >= 2:
                        sem0, sem = ((spa0, cx.spa) if vb % 2 == 0
                                     else (spb0, cx.spb))
                        s.wait_ge(sem.h, sem0 + 16 * (vb // 2))
                    cx.ss.inc(s.activation(vsb[:, vb % 2, 0:512],
                                           cx.pb[pbb][:, 0:512], act))
                    cx.ss.inc(s.activation(vsb[:, vb % 2, 512:1024],
                                           cx.pb[pbb + 1][:, 0:512], act))
                else:
                    cx.ss.inc(s.activation(pfx_dst[0:32, 0:512],
                                           cx.pb[pbb][0:32, 0:512], act))
                    cx.ss.inc(s.activation(pfx_dst[0:32, 512:1024],
                                           cx.pb[pbb + 1][0:32, 0:512], act))

        @B.sync
        def _(sync):
            dma_waits(cx, sync)
            eng = cx.ss if act is not None else cx.sv
            for vb in range(NBLK):
                sync.wait_ge(eng.h, ev[vb])
                sem = cx.spa if vb % 2 == 0 else cx.spb
                sem.inc(sync.dma_start(
                    out=bass.AP(out_d, 128 * vb * C, [[C, P], [1, C]]),
                    in_=vsb[:, vb % 2, :]), 16)


def _b_exch(cx):
    nc = cx.nc
    bx = cx.bx
    with blk(cx, "bexch") as B:
        scc0 = cx.scc.v
        sd0 = cx.sd.v
        sgm0 = cx.sgm.v
        zbuf = bx["vsb"]

        @B.gpsimd
        def _(g):
            dma_waits(cx, g)
            cx.sgm.inc(g.memset(zbuf[:, 0, :], 0.0))
            cx.scc.inc(g.collective_compute(
                "AllGather", ALU.bypass,
                replica_groups=[[0, 1], [2, 3], [4, 5], [6, 7]],
                ins=[bx["cc_k_in"].ap()], outs=[bx["gath_k"].ap()]), 1)
            cx.scc.inc(g.collective_compute(
                "AllGather", ALU.bypass,
                replica_groups=[[0, 1], [2, 3], [4, 5], [6, 7]],
                ins=[bx["cc_v_in"].ap()], outs=[bx["gath_v"].ap()]), 1)

        @B.sync
        def _(sync):
            dma_waits(cx, sync)
            sync.wait_ge(cx.sgm.h, sgm0 + 1)
            for zb in range(24):
                cx.sd.inc(sync.dma_start(
                    out=bass.AP(bx["vpad"], zb * P * C, [[C, P], [1, C]]),
                    in_=zbuf[:, 0, :]), 16)
            sync.wait_ge(cx.scc.h, scc0 + 2)
            sync.wait_ge(cx.sd.h, sd0 + 24 * 16)
            hf0, hf1 = _hf_conds(sync)
            cx.sd.inc(sync.dma_start(
                out=bx["kwin"][:, :, 3072:5120],
                in_=gk_ap(bx["gath_k"], 0, 0, 2048), cond=hf0), 16)
            cx.sd.inc(sync.dma_start(
                out=bx["kwin"][:, :, 1024:3072],
                in_=gk_ap(bx["gath_k"], 0, 0, 2048), cond=hf1), 16)
            cx.sd.inc(sync.dma_start(
                out=bx["kwin"][:, :, 3072:5120],
                in_=gk_ap(bx["gath_k"], 1, 0, 2048), cond=hf1), 16)
            cx.sd.inc(sync.dma_start(
                out=bass.AP(bx["vpad"], 24 * P * C, [[C, 16 * P], [1, C]]),
                in_=gv_ap(bx["gath_v"], 0, 0, 16 * P), cond=hf0), 16)
            cx.sd.inc(sync.dma_start(
                out=bass.AP(bx["vpad"], 8 * P * C, [[C, 16 * P], [1, C]]),
                in_=gv_ap(bx["gath_v"], 0, 0, 16 * P), cond=hf1), 16)
            cx.sd.inc(sync.dma_start(
                out=bass.AP(bx["vpad"], 24 * P * C, [[C, 16 * P], [1, C]]),
                in_=gv_ap(bx["gath_v"], 1, 0, 16 * P), cond=hf1), 16)


def gk_ap(gath_k, rank, t0, w):
    return bass.AP(gath_k, rank * C * 2048 + t0,
                   [[2048, P], [P * 2048, NCH], [1, w]])


def gv_ap(gath_v, rank, row0, nrow):
    return bass.AP(gath_v, rank * 2048 * C + row0 * C, [[C, nrow], [1, C]])


VS_R0 = [0, 1, 2, 3, 4, 6]
VS_R1 = [8, 12, 16, 24]


def _b_tile(cx, t):
    nc = cx.nc
    bx = cx.bx
    q_sb, kwin, prodb = bx["q_sb"], bx["kwin"], bx["prod"]
    ebrow, rtm, zstb = bx["ebrow"], bx["rtm"], bx["zst"]
    tmpb = bx["vsb"]
    vstr, esb, ptb = bx["vstr"], bx["esb"], bx["ptb"]

    with blk(cx, "btq") as B:
        st0, ss0, sv0, sd0 = cx.st.v, cx.ss.v, cx.sv.v, cx.sd.v
        sga0, sgb0 = cx.sga.v, cx.sgb.v
        pfx = (t == 3)
        st_q = {}
        n = st0
        for dq in range(NCH):
            n += 8
            st_q[dq] = n
            if pfx:
                n += 8
                st_q[(dq, "p")] = n
        st_sc = {}
        st_r = {}
        for ji in range(NSP):
            for kc in range(NCH):
                n += 1
                st_sc[(ji, kc)] = n
            n += 4
            st_r[ji] = n
        sv_qe = {}
        n = sv0
        for dq in range(NCH):
            n += 1
            sv_qe[dq] = n
            if pfx:
                n += 1
        sv_pr = {}
        sv_r = {}
        for ji in range(NSP):
            for kc in range(NCH):
                n += 1
                sv_pr[(ji, kc)] = n
            n += 8
            sv_r[ji] = n
        ss_eb = {ji: ss0 + ji + 1 for ji in range(NSP)}

        @B.gpsimd
        def _(g):
            dma_waits(cx, g)
            for dq in range(NCH):
                if dq >= 2:
                    g.wait_ge(cx.st.h,
                              st_q[(dq - 2, "p")] if pfx else st_q[dq - 2])
                sem = cx.sga if dq % 2 == 0 else cx.sgb
                sem.inc(g.dma_start(
                    out=bx["wstr"][:, dq % 2, :, :],
                    in_=pm_dram(cx.din["Wqkv"], NCH, 3072, 128 * dq, 128)), 16)

        @B.sync
        def _(sync):
            dma_waits(cx, sync)
            cx.sd.inc(sync.dma_start(
                out=bx["gate_sb"][:, :, :],
                in_=bass.AP(bx["gate_d"], 512 * t * C,
                            [[C, P], [P * C, 4], [1, C]])), 16)

        @B.tensor
        def _(t_):
            dma_waits(cx, t_)
            for dq in range(NCH):
                sem0, sem = ((sga0, cx.sga) if dq % 2 == 0
                             else (sgb0, cx.sgb))
                t_.wait_ge(sem.h, sem0 + 16 * (dq // 2 + 1))
                if dq >= 2:
                    t_.wait_ge(cx.sv.h, sv_qe[dq - 2] + (1 if pfx else 0))
                for kc in range(NCH):
                    cx.st.inc(t_.matmul(
                        cx.pb[dq % 2][:, 0:512], bx["wstr"][:, dq % 2, kc, :],
                        cx.xhat[:, kc, 384 + 512 * t:384 + 512 * (t + 1)],
                        start=(kc == 0), stop=(kc == NCH - 1)))
                if pfx:
                    for kc in range(NCH):
                        cx.st.inc(t_.matmul(
                            cx.pb[2 + dq % 2][:, 0:32],
                            bx["wstr"][:, dq % 2, kc, :],
                            cx.xhat[:, kc, 1:33],
                            start=(kc == 0), stop=(kc == NCH - 1)))
            for ji, j in enumerate(SPARSE_J):
                w0 = 3072 - 128 * j + 512 * t
                for kc in range(NCH):
                    t_.wait_ge(cx.sv.h, sv_pr[(ji, kc)])
                    cx.st.inc(t_.matmul(
                        cx.pb[5][0:H, 0:512], cx.hindb_sb[:, kc, :],
                        prodb[:, kc % 2, :],
                        start=(kc == 0), stop=(kc == NCH - 1)))
                t_.wait_ge(cx.ss.h, ss_eb[ji])
                if ji:
                    t_.wait_ge(cx.sv.h, sv_r[ji - 1])
                for mi in range(4):
                    cx.st.inc(t_.matmul(
                        cx.pb[6][:, H * mi:H * (mi + 1)],
                        ebrow[:, 128 * mi:128 * (mi + 1)], bx["id16"][:, :],
                        start=(mi == 0), stop=(mi == 3)))

        @B.vector
        def _(v):
            dma_waits(cx, v)
            for dq in range(NCH):
                v.wait_ge(cx.st.h, st_q[dq])
                cx.sv.inc(v.tensor_scalar_add(
                    q_sb[:, dq, :], cx.pb[dq % 2][:, 0:512],
                    bx["bqt"][:, dq:dq + 1]))
                if pfx:
                    v.wait_ge(cx.st.h, st_q[(dq, "p")])
                    cx.sv.inc(v.tensor_scalar_add(
                        bx["qp_sb"][:, dq, :], cx.pb[2 + dq % 2][:, 0:32],
                        bx["bqt"][:, dq:dq + 1]))
            for ji, j in enumerate(SPARSE_J):
                w0 = 3072 - 128 * j + 512 * t
                for kc in range(NCH):
                    if (ji, kc) != (0, 0) and (ji * NCH + kc) >= 2:
                        pi = ji * NCH + kc - 2
                        v.wait_ge(cx.st.h, st_sc[divmod(pi, NCH)])
                    cx.sv.inc(v.tensor_mul(
                        prodb[:, kc % 2, :], q_sb[:, kc, :],
                        kwin[:, kc, w0:w0 + 512]))
                v.wait_ge(cx.st.h, st_r[ji])
                for mi in range(4):
                    cx.sv.inc(v.tensor_copy(rtm[:, mi, ji, :],
                                            cx.pb[6][:, H * mi:H * (mi + 1)]))
                    if ji == 0:
                        cx.sv.inc(v.tensor_copy(
                            zstb[:, mi, :], cx.pb[6][:, H * mi:H * (mi + 1)]))
                    else:
                        cx.sv.inc(v.tensor_add(
                            zstb[:, mi, :], zstb[:, mi, :],
                            cx.pb[6][:, H * mi:H * (mi + 1)]))

        @B.scalar
        def _(s):
            dma_waits(cx, s)
            for ji, j in enumerate(SPARSE_J):
                s.wait_ge(cx.st.h, st_sc[(ji, NCH - 1)])
                ci = DOFFS.index(128 * j)
                cx.ss.inc(s.activation(ebrow[:, 0:512], cx.pb[5][0:H, 0:512],
                                       AF.Exp, bias=bx["lnbt"][:, ci:ci + 1],
                                       scale=SCL))

    _b_blocks(cx, t)

    _b_wout(cx, t, bx["g3fm"], 384 + 512 * t, 512)


JSEQ = [3, 4, 6, 8, 12, 16, 24]


def _b_blocks(cx, t):
    nc = cx.nc
    bx = cx.bx
    vstr, rtm, tmpb = bx["vstr"], bx["rtm"], bx["tmpb"]
    esb, ptb = bx["esb"], bx["ptb"]
    kwin, q_sb, zstb = bx["kwin"], bx["q_sb"], bx["zst"]

    with blk(cx, "btm") as B:
        st0, ss0, sv0, sd0, sgm0 = (cx.st.v, cx.ss.v, cx.sv.v, cx.sd.v,
                                    cx.sgm.v)

        def slot_blocks(mi, r):
            out = []
            deltas = VS_R0 if r == 0 else VS_R1
            for si, dj in enumerate(deltas):
                ents = []
                for hf in range(2):
                    bi = hf * 16 + 4 * t + mi - dj
                    if bi >= 0:
                        ents.append((hf, bi))
                out.append((si, ents))
            return out

        sd_fill = {}
        n = sd0
        for mi in range(4):
            for r in range(2):
                n += 16 * (len(VS_R0) if r == 0 else len(VS_R1))
                sd_fill[(mi, r)] = n
        if MK_DBG == "bt0" and t == 0:
            cx._bt0_sg = cx.sg.v + 8 * 16

        st_id = {}
        st_st = {}
        st_av = {}
        st_z = {}
        st_tr = {}
        n = st0
        for mi in range(4):
            for j in (3, 4, 6):
                n += 2
                st_id[(mi, j)] = n
            for h in range(H):
                n += 3
                st_st[(mi, h)] = n
                n += 6
                st_av[(mi, h)] = n
            st_z[mi] = n
            for j in VS_R1:
                n += 2
                st_id[(mi, j)] = n
            st_acc = n
            st_id[(mi, "acc")] = st_acc
            n += 8
            st_tr[mi] = n

        sv_tmp = {}
        sv_mask = {}
        sv_norm = {}
        sv_tev = {}
        n = sv0
        for mi in range(4):
            for j in (3, 4, 6):
                n += 1
                sv_tmp[(mi, j)] = n
            for h in range(H):
                n += 1
                sv_mask[(mi, h)] = n
            n += 1
            for idx, j in enumerate(VS_R1):
                n += 1
                sv_tmp[(mi, j)] = n
                if idx == 0:
                    n += 1
            n += 3
            sv_norm[mi] = n
            n += 8
            sv_tev[mi] = n

        ss_exp = {}
        n = ss0
        for mi in range(4):
            for h in range(H):
                n += 1
                ss_exp[(mi, h)] = n

        def hg(mi, h):
            return mi * H + h

        @B.sync
        def _(sync):
            dma_waits(cx, sync)
            for mi in range(4):
                for r in range(2):
                    if r == 0 and mi > 0:
                        sync.wait_ge(cx.sv.h, sv_tmp[(mi - 1, 24)])
                        sync.wait_ge(cx.st.h, st_av[(mi - 1, H - 1)])
                    if r == 1:
                        sync.wait_ge(cx.st.h, st_av[(mi, H - 1)])
                        sync.wait_ge(cx.sv.h, sv_tmp[(mi, 6)])
                    deltas = VS_R0 if r == 0 else VS_R1
                    for si, dj in enumerate(deltas):
                        rb = 4 * t + mi - dj + 24
                        cx.sd.inc(sync.dma_start(
                            out=vstr[:, si, :],
                            in_=bass.AP(bx["vpad"], rb * P * C,
                                        [[C, P], [1, C]])), 16)

        @B.tensor
        def _(t_):
            dma_waits(cx, t_)
            for mi in range(4):
                first = True
                for j in (3, 4, 6):
                    t_.wait_ge(cx.sv.h, sv_tmp[(mi, j)])
                    if first and mi > 0:
                        t_.wait_ge(cx.sv.h, sv_norm[mi - 1])
                    js = JSEQ.index(j)
                    for half in range(2):
                        cx.st.inc(t_.matmul(
                            cx.pb[half][:, 0:512], cx.ident_bf[:, :],
                            tmpb[:, js % 2, 512 * half:512 * (half + 1)],
                            start=first, stop=False))
                    first = False
                for h in range(H):
                    bank = cx.pb[3 + h % 2]
                    if hg(mi, h) >= 2:
                        mi2, h2 = divmod(hg(mi, h) - 2, H)
                        t_.wait_ge(cx.ss.h, ss_exp[(mi2, h2)])
                    base = 64 * (h % 2)
                    qs = q_sb[base:base + 64, h // 2, 128 * mi:128 * (mi + 1)]
                    for b in range(3):
                        kcol = 3072 + 128 * (4 * t + mi - b)
                        cx.st.inc(t_.matmul(
                            bank[:, 128 * b:128 * (b + 1)],
                            kwin[base:base + 64, h // 2, kcol:kcol + 128],
                            qs, start=(b == 0), stop=(b == 2)))
                    t_.wait_ge(cx.sv.h, sv_mask[(mi, h)])
                    for b in range(3):
                        lhs = ptb[:, hg(mi, h) % 2, 128 * b:128 * (b + 1)]
                        cx.st.inc(t_.matmul(
                            cx.pb[h // 8][:, 64 * (h % 8):64 * (h % 8) + 64],
                            lhs, vstr[:, b, 64 * h:64 * (h + 1)],
                            start=False, stop=False))
                    for b in range(3):
                        lhs = ptb[:, hg(mi, h) % 2, 128 * b:128 * (b + 1)]
                        cx.st.inc(t_.matmul(
                            cx.pb[2][:, h:h + 1], lhs, bx["onesb"][:, :],
                            start=(h == 0 and b == 0),
                            stop=(h == H - 1 and b == 2)))
                for j in VS_R1:
                    t_.wait_ge(cx.sv.h, sv_tmp[(mi, j)])
                    js = JSEQ.index(j)
                    for half in range(2):
                        cx.st.inc(t_.matmul(
                            cx.pb[half][:, 0:512], cx.ident_bf[:, :],
                            tmpb[:, js % 2, 512 * half:512 * (half + 1)],
                            start=False, stop=(j == 24)))
                t_.wait_ge(cx.sv.h, sv_norm[mi])
                if mi > 0:
                    t_.wait_ge(cx.sv.h, sv_tev[mi - 1])
                for dc in range(NCH):
                    bank = cx.pb[6 + dc // 4]
                    cx.st.inc(t_.matmul(
                        bank[:, 128 * (dc % 4):128 * (dc % 4 + 1)],
                        bx["g3tm"][:, 128 * dc:128 * (dc + 1)],
                        cx.ident_bf[:, :],
                        start=(dc % 4 == 0), stop=(dc % 4 == 3)))

        @B.vector
        def _(v):
            dma_waits(cx, v)

            def _tmp_mul(mi, j):
                js = JSEQ.index(j)
                si = (3 + (3, 4, 6).index(j)) if j in (3, 4, 6) \
                    else VS_R1.index(j)
                r = 0 if j in (3, 4, 6) else 1
                v.wait_ge(cx.sd.h, sd_fill[(mi, r)])
                gi = mi * 7 + js
                if gi >= 2:
                    pm, pj = divmod(gi - 2, 7)
                    v.wait_ge(cx.st.h, st_id[(pm, JSEQ[pj])])
                cx.sv.inc(v.tensor_mul(
                    tmpb[:, js % 2, :].rearrange(
                        "p (a b) -> p a b", a=H),
                    vstr[:, si, :].rearrange("p (a b) -> p a b", a=H),
                    rtm[:, mi, js:js + 1, :].rearrange(
                        "p a b -> p b a").broadcast_to([P, H, 64])))

            for mi in range(4):
                for j in (3, 4, 6):
                    _tmp_mul(mi, j)
                for h in range(H):
                    v.wait_ge(cx.ss.h, ss_exp[(mi, h)])
                    if hg(mi, h) >= 2:
                        mi2, h2 = divmod(hg(mi, h) - 2, H)
                        v.wait_ge(cx.st.h, st_av[(mi2, h2)])
                    cx.sv.inc(v.tensor_mul(
                        ptb[:, hg(mi, h) % 2, :],
                        esb[:, hg(mi, h) % 2, :],
                        bx["btoep"][:, h, :]))
                v.wait_ge(cx.st.h, st_av[(mi, H - 1)])
                cx.sv.inc(v.tensor_add(bx["zt"][:, :], zstb[:, mi, :],
                                       cx.pb[2][:, 0:H]))
                for idx, j in enumerate(VS_R1):
                    _tmp_mul(mi, j)
                    if idx == 0:
                        cx.sv.inc(v.reciprocal(bx["zr"][:, :], bx["zt"][:, :]))
                if MK_DBG == "bt0" and t == 0 and mi == 1:
                    v.wait_ge(cx.sg.h, cx._bt0_sg)
                v.wait_ge(cx.st.h, st_id[(mi, "acc")])
                for half in range(2):
                    cx.sv.inc(v.tensor_mul(
                        bx["t1"][:, 512 * half:512 * (half + 1)].rearrange(
                            "p (a b) -> p a b", a=8),
                        cx.pb[half][:, 0:512].rearrange(
                            "p (a b) -> p a b", a=8),
                        bx["zr"][:, 8 * half:8 * (half + 1)].unsqueeze(
                            2).broadcast_to([P, 8, 64])))
                cx.sv.inc(v.tensor_mul(bx["g3tm"][:, :], bx["t1"][:, :],
                                       bx["gate_sb"][:, mi, :]))
                v.wait_ge(cx.st.h, st_tr[mi])
                for dc in range(NCH):
                    bank = cx.pb[6 + dc // 4]
                    cx.sv.inc(v.tensor_copy(
                        bx["g3fm"][:, dc, 128 * mi:128 * (mi + 1)],
                        bank[:, 128 * (dc % 4):128 * (dc % 4 + 1)]))

        @B.scalar
        def _(s):
            dma_waits(cx, s)
            for mi in range(4):
                for h in range(H):
                    s.wait_ge(cx.st.h, st_st[(mi, h)])
                    if hg(mi, h) >= 2:
                        mi2, h2 = divmod(hg(mi, h) - 2, H)
                        s.wait_ge(cx.sv.h, sv_mask[(mi2, h2)])
                    cx.ss.inc(s.activation(
                        esb[:, hg(mi, h) % 2, :], cx.pb[3 + h % 2][:, 0:384],
                        AF.Exp, bias=cx.cconst[:, 3:4], scale=SCL))

        if MK_DBG == "bt0" and t == 0:
            @B.gpsimd
            def _(g):
                dma_waits(cx, g)
                g.wait_ge(cx.sv.h, sv_norm[0])
                for i, (src, w) in enumerate((
                        (bx["zt"][:, :], H), (bx["zr"][:, :], H),
                        (zstb[:, 0, :], H), (bx["t1"][:, :], C),
                        (bx["g3tm"][:, :], C),
                        (bx["gate_sb"][:, 0, :], C),
                        (bx["rtm"][:, 0, :, :].rearrange(
                            "p a b -> p (a b)"), NSP * H),
                        (bx["q_sb"][:, 0, :], 512))):
                    cx.sg.inc(g.dma_start(
                        out=bass.AP(cx.dbg, i * P * TP, [[TP, P], [1, w]]),
                        in_=src), 16)
                g.wait_ge(cx.sg.h, cx.sg.v)


def _hf_conds(eng):
    pid = eng.partition_id()
    return ((pid + 1) % 2, pid % 2)


def _b_wout(cx, t, g3src, col0, W):
    nc = cx.nc
    bx = cx.bx
    prefix = (col0 == 0)
    with blk(cx, "bwo") as B:
        st0, sv0, sga0, sgb0 = cx.st.v, cx.sv.v, cx.sga.v, cx.sgb.v
        spa0, spb0 = cx.spa.v, cx.spb.v
        st_g = {do: st0 + 8 * (do + 1) for do in range(NCH)}
        sv_e = {do: sv0 + do + 1 for do in range(NCH)}

        @B.gpsimd
        def _(g):
            dma_waits(cx, g)
            for do in range(NCH):
                if do >= 2:
                    g.wait_ge(cx.st.h, st_g[do - 2])
                sem = cx.sga if do % 2 == 0 else cx.sgb
                sem.inc(g.dma_start(
                    out=bx["wstr"][:, do % 2, :, :],
                    in_=pm_dram(cx.din["WoutT"], NCH, C, 128 * do, 128)), 16)

        @B.tensor
        def _(t_):
            dma_waits(cx, t_)
            for do in range(NCH):
                sem0, sem = ((sga0, cx.sga) if do % 2 == 0
                             else (sgb0, cx.sgb))
                t_.wait_ge(sem.h, sem0 + 16 * (do // 2 + 1))
                if do >= 2:
                    t_.wait_ge(cx.sv.h, sv_e[do - 2])
                for kc in range(NCH):
                    cx.st.inc(t_.matmul(
                        cx.pb[do % 2][:, 0:W], bx["wstr"][:, do % 2, kc, :],
                        g3src[:, kc, 0:W],
                        start=(kc == 0), stop=(kc == NCH - 1)))

        @B.vector
        def _(v):
            dma_waits(cx, v)
            for do in range(NCH):
                v.wait_ge(cx.st.h, st_g[do])
                if prefix:
                    cx.sv.inc(v.tensor_scalar_add(
                        bx["opfx"][:, do, 1:33], cx.pb[do % 2][:, 0:32],
                        bx["bot"][:, do:do + 1]))
                else:
                    if do >= 2:
                        sem0, sem = ((spa0, cx.spa) if do % 2 == 0
                                     else (spb0, cx.spb))
                        v.wait_ge(sem.h, sem0 + 16 * (do // 2))
                    cx.sv.inc(v.tensor_scalar_add(
                        bx["osb"][:, do % 2, 0:W], cx.pb[do % 2][:, 0:W],
                        bx["bot"][:, do:do + 1]))

        @B.sync
        def _(sync):
            dma_waits(cx, sync)
            if prefix:
                sync.wait_ge(cx.sv.h, sv_e[NCH - 1])
                cx.sp.inc(sync.dma_start(
                    out=pm_dram(cx.f_out, NCH, TP, 0, 64),
                    in_=bx["opfx"][:, :, :]), 16)
                return
            for do in range(NCH):
                sync.wait_ge(cx.sv.h, sv_e[do])
                sem = cx.spa if do % 2 == 0 else cx.spb
                sem.inc(sync.dma_start(
                    out=bass.AP(cx.f_out, do * P * TP + col0,
                                [[TP, P], [1, W]]),
                    in_=bx["osb"][:, do % 2, 0:W]), 16)


def _b_prefix(cx):
    nc = cx.nc
    bx = cx.bx
    kwp, vwp, esb, ptb = bx["kwp"], bx["vwp"], bx["esb"], bx["ptb"]
    with blk(cx, "bpfx") as B:
        st0, ss0, sv0 = cx.st.v, cx.ss.v, cx.sv.v
        st_st = {h: st0 + 3 * h + 1 for h in range(H)}
        st_av = {h: st0 + 3 * h + 3 for h in range(H)}
        st_acc = st0 + 3 * H
        st_tr = st_acc + NCH
        ss_exp = {h: ss0 + h + 1 for h in range(H)}
        sv_mask = {h: sv0 + h + 1 for h in range(H)}
        sv_norm = sv0 + H + 7
        sv_tev = sv_norm + NCH

        @B.tensor
        def _(t_):
            dma_waits(cx, t_)
            for h in range(H):
                base = 64 * (h % 2)
                bank = cx.pb[3 + h % 2]
                if h >= 2:
                    t_.wait_ge(cx.ss.h, ss_exp[h - 2])
                cx.st.inc(t_.matmul(
                    bank[0:64, 0:32], kwp[base:base + 64, h // 2, :],
                    bx["qp_sb"][base:base + 64, h // 2, :],
                    start=True, stop=True))
                t_.wait_ge(cx.sv.h, sv_mask[h])
                lhs = ptb[0:64, h % 2, 0:32]
                cx.st.inc(t_.matmul(
                    cx.pb[h // 8][0:32, 64 * (h % 8):64 * (h % 8) + 64],
                    lhs, vwp[:, 64 * h:64 * (h + 1)],
                    start=(h % 8 == 0), stop=(h % 8 == 7)))
                cx.st.inc(t_.matmul(
                    cx.pb[2][0:32, h:h + 1], lhs, bx["onesb"][0:64, :],
                    start=(h == 0), stop=(h == H - 1)))
            t_.wait_ge(cx.sv.h, sv_norm)
            for dc in range(NCH):
                bank = cx.pb[6 + dc // 4]
                cx.st.inc(t_.matmul(
                    bank[:, 32 * (dc % 4):32 * (dc % 4 + 1)],
                    bx["g3tm"][0:32, 128 * dc:128 * (dc + 1)],
                    cx.ident_bf[0:32, 0:32],
                    start=(dc % 4 == 0), stop=(dc % 4 == 3)))

        @B.scalar
        def _(s):
            dma_waits(cx, s)
            for h in range(H):
                s.wait_ge(cx.st.h, st_st[h])
                if h >= 2:
                    s.wait_ge(cx.sv.h, sv_mask[h - 2])
                cx.ss.inc(s.activation(
                    esb[0:64, h % 2, 0:32], cx.pb[3 + h % 2][0:64, 0:32],
                    AF.Exp, bias=cx.cconst[0:64, 3:4], scale=SCL))

        @B.vector
        def _(v):
            dma_waits(cx, v)
            for h in range(H):
                v.wait_ge(cx.ss.h, ss_exp[h])
                if h >= 2:
                    v.wait_ge(cx.st.h, st_av[h - 2])
                cx.sv.inc(v.tensor_mul(
                    ptb[0:64, h % 2, 0:32], esb[0:64, h % 2, 0:32],
                    bx["btp"][:, h, :]))
            v.wait_ge(cx.st.h, st_acc)
            cx.sv.inc(v.tensor_add(bx["zt"][0:32, :], bx["zct"][:, :],
                                   cx.pb[2][0:32, 0:H]))
            cx.sv.inc(v.memset(bx["opfx"][:, :, :], 0.0))
            cx.sv.inc(v.reciprocal(bx["zr"][0:32, :], bx["zt"][0:32, :]))
            cx.sv.inc(v.memset(bx["opfx"][:, :, :], 0.0))
            for half in range(2):
                cx.sv.inc(v.tensor_mul(
                    bx["t1"][0:32, 512 * half:512 * (half + 1)].rearrange(
                        "p (a b) -> p a b", a=8),
                    cx.pb[half][0:32, 0:512].rearrange(
                        "p (a b) -> p a b", a=8),
                    bx["zr"][0:32, 8 * half:8 * (half + 1)].unsqueeze(
                        2).broadcast_to([32, 8, 64])))
            cx.sv.inc(v.tensor_mul(bx["g3tm"][0:32, :], bx["t1"][0:32, :],
                                   bx["gatep"][:, :]))
            v.wait_ge(cx.st.h, st_tr)
            for dc in range(NCH):
                bank = cx.pb[6 + dc // 4]
                cx.sv.inc(v.tensor_copy(
                    bx["g3fm"][:, dc, 0:32],
                    bank[:, 32 * (dc % 4):32 * (dc % 4 + 1)]))

    _b_wout(cx, None, bx["g3fm"], 0, 32)


def ph_store(cx):
    nc = cx.nc
    with blk(cx, "store") as B:
        @B.sync
        def _(sync):
            dma_waits(cx, sync)
            cx.sp.inc(sync.dma_start(
                out=pm_dram(cx.out_x, NCH, OWNW, 0, OWNW),
                in_=cx.x_res[:, :, OWN0:OWN0 + OWNW]), 16)
            sync.wait_ge(cx.sp.h, cx.sp.v)


_CACHE = {}


def _prep_core(c, full):
    b, hf = c // 2, c % 2
    own0 = hf * 2048
    x = full["L0_attn_a_x"]
    xsrc = np.zeros((TP, C), np.float32)
    xsrc[PFX0:PFX0 + PFXW] = x[b, 0:PFXW]
    if hf == 1:
        xsrc[33:384] = x[b, own0 - NW:own0]
    xsrc[OWN0:OWN0 + OWNW] = x[b, own0:own0 + OWNW]
    d = {"x_fm": np.ascontiguousarray(xsrc.T.reshape(NCH, P, TP))}
    for nm, key in (("Wr", "L0_attn_a_Wr"), ("Wk", "L0_attn_a_Wk"),
                    ("Wv", "L0_attn_a_Wv"), ("Wo", "L0_attn_a_Wo")):
        d[nm] = np.ascontiguousarray(full[key].reshape(NCH, P, C))
    for nm, key in (("w1", "L0_attn_a_w1"), ("a1", "L0_attn_a_a1"),
                    ("g1", "L0_attn_a_g1")):
        d[nm] = np.ascontiguousarray(full[key].reshape(NCH, P, 64))
    d["w2"] = np.ascontiguousarray(full["L0_attn_a_w2"])
    d["a2"] = np.ascontiguousarray(full["L0_attn_a_a2"])
    d["g2"] = np.ascontiguousarray(full["L0_attn_a_g2"])
    colsin = np.zeros((P, 10, NCH), np.float32)
    for i, key in enumerate(("L0_attn_a_x_r", "L0_attn_a_x_w", "L0_attn_a_x_k",
                             "L0_attn_a_x_v", "L0_attn_a_x_a", "L0_attn_a_x_g",
                             "L0_attn_a_k_k", "L0_attn_a_r_k", "L0_attn_a_ln_w",
                             "L0_attn_a_ln_b")):
        colsin[:, i, :] = full[key].reshape(C).reshape(NCH, P).T
    d["colsin"] = np.ascontiguousarray(colsin)
    d["w0col"] = np.ascontiguousarray(full["L0_attn_a_w0"].reshape(H, 1))
    d["a0col"] = np.ascontiguousarray(full["L0_attn_a_a0"].reshape(H, 1))
    if N_SUB >= 2:
        d["W1T"] = np.ascontiguousarray(full["L0_ffn_W1"].T.reshape(NCH, P, 4096))
        d["W2T"] = np.ascontiguousarray(full["L0_ffn_W2"].T.reshape(32, P, C))
        d["b1col"] = np.ascontiguousarray(full["L0_ffn_b1"].reshape(32, P).T)
        d["b2row"] = np.ascontiguousarray(full["L0_ffn_b2"].reshape(NCH, 1, P))
        d["fscale"] = np.ascontiguousarray(full["L0_ffn_scale"].reshape(1, 1))
    if N_SUB >= 3:
        d["Wqkv"] = np.ascontiguousarray(full["L0_attn_b_w_qkv"].reshape(NCH, P, 3072))
        d["Wgate"] = np.ascontiguousarray(full["L0_attn_b_w_gate"].reshape(NCH, P, C))
        d["WoutT"] = np.ascontiguousarray(full["L0_attn_b_w_out"].reshape(NCH, P, C))
        d["bqkv_row"] = np.ascontiguousarray(full["L0_attn_b_b_qkv"].reshape(24, 1, P))
        d["bqkvT"] = np.ascontiguousarray(full["L0_attn_b_b_qkv"].reshape(24, P).T)
        d["boutT"] = np.ascontiguousarray(full["L0_attn_b_b_out"].reshape(NCH, P).T)
        d["bgate_row"] = np.ascontiguousarray(full["L0_attn_b_b_gate"].reshape(NCH, 1, P))
        d["pos_bias"] = np.ascontiguousarray(full["L0_attn_b_pos_bias"])
    return d


def kernel(**inputs):
    key = "nc%d_%d_%s" % (N_SUB, MK_PH, MK_DBG)
    if key not in _CACHE:
        _CACHE[key] = build_nc()
    nc = _CACHE[key]
    in_maps = [_prep_core(c, inputs) for c in range(8)]
    res = run_bass_kernel_spmd(nc, in_maps, core_ids=list(range(8)))
    kernel.LAST = res
    out = np.zeros((4, 4096, C), np.float32)
    for c in range(8):
        b, hf = c // 2, c % 2
        o = res.results[c]["out_x"]
        out[b, hf * 2048:(hf + 1) * 2048] = o.reshape(C, OWNW).T
    return out
